# version 24 (final)
# speedup vs baseline: 1.0293x; 1.0055x over previous
import contextlib
import os as _os
import numpy as np
import concourse.bass as bass
import concourse.mybir as mybir
from concourse.bass_utils import run_bass_kernel_spmd

F32 = mybir.dt.float32
BF16 = mybir.dt.bfloat16
I32 = mybir.dt.int32
AF = mybir.ActivationFunctionType
ALU = mybir.AluOpType
AX = mybir.AxisListType

NT = 2048
D = 1024
NCH = 8
TT = 512
NTT = NT // TT
HD = 64
NH = 16
NKV = 4
FFN = 2048
E = 8
EPS = 1e-5
SCALE = HD ** -0.5
DIL, BLOCK_M = 8, 64
KN = 256

SPARSE = True
NTILES = 15
NSLOT = NTILES * TT

Q_CHUNK_HEADS = [(8 * (cj // 4) + (cj % 4), 8 * (cj // 4) + (cj % 4) + 4)
                 for cj in range(8)]

OFF_ACC = 0
OFF_XN = 65536
OFF_A = 98304
OFF_B = 131072
OFF_KV = 163840
OFF_POOL = 196864
OFF_MISC = 205056
ARENA_SZ = 207360


class Prog:
    ENGINES = ("tensor", "vector", "scalar", "gpsimd", "sync")
    ENG_SEM = {"tensor": "T", "vector": "V", "scalar": "S", "gpsimd": "G",
               "sync": "Y"}

    def __init__(self, nc):
        self.nc = nc
        self.q = {e: [] for e in self.ENGINES}
        self.cnt = {}
        self.sems = {}
        self.marks = {e: {} for e in self.ENGINES}
        self.dma_pool = {"gpsimd": [f"dG{i}" for i in range(12)],
                         "sync": [f"dS{i}" for i in range(8)],
                         "scalar": [f"dA{i}" for i in range(4)]}
        self.dma_rr = {k: 0 for k in self.dma_pool}
        self.dma_uses = {}
        self.small_tok = {}

    def all_sem_names(self):
        names = list(self.ENG_SEM.values())
        for v in self.dma_pool.values():
            names.extend(v)
        return names

    @staticmethod
    def _is_dma(sem):
        return sem.startswith("d")

    def wait(self, eng, tok):
        if tok is None:
            return
        sem, val = tok
        if self.marks[eng].get(sem, 0) >= val:
            return
        self.marks[eng][sem] = val
        self.q[eng].append(["w", sem, val])

    CHAINED = ("vector", "scalar", "gpsimd")

    def run(self, eng, fn, waits=(), inc=None, sm=False, nochain=False):
        if eng in self.CHAINED:
            if not nochain:
                lt = self.small_tok.get(eng)
                if lt is not None:
                    self.wait(eng, lt)
            inc = inc or self.ENG_SEM[eng]
        for t in waits:
            self.wait(eng, t)
        tok = None
        if inc is not None:
            amt = 16 if self._is_dma(inc) else 1
            self.cnt[inc] = self.cnt.get(inc, 0) + amt
            tok = (inc, self.cnt[inc])
        self.q[eng].append(["r", fn, tok])
        if eng in self.CHAINED:
            self.small_tok[eng] = tok
        return tok

    def dma(self, eng, fn, waits=()):
        pool = self.dma_pool[eng]
        sem = pool[self.dma_rr[eng] % len(pool)]
        self.dma_rr[eng] += 1
        u = self.dma_uses.get(sem, 0)
        if u > 0:
            self.wait(eng, (sem, 16 * u))
        self.dma_uses[sem] = u + 1
        return self.run(eng, fn, waits, inc=sem)

    def mark(self, eng):
        for it in reversed(self.q[eng]):
            if it[0] == "r":
                if it[2] is None:
                    sem = self.ENG_SEM[eng]
                    self.cnt[sem] = self.cnt.get(sem, 0) + 1
                    it[2] = (sem, self.cnt[sem])
                return it[2]
        return None

    def emit(self, block):
        for eng in self.ENGINES:
            items = self.q[eng]

            def make_body(items):
                def body(g):
                    for it in items:
                        if it[0] == "w":
                            g.wait_ge(self.sems[it[1]], it[2])
                        else:
                            inst = it[1](g)
                            if it[2] is not None:
                                sem = it[2][0]
                                inst.then_inc(self.sems[sem],
                                              16 if self._is_dma(sem) else 1)
                return body
            getattr(block, eng)(make_body(items))


def build_program(phases=99, dbg=None, sub=9):
    nc = bass.Bass("TRN2", target_bir_lowering=False)
    dbg = dbg or []

    dr = {}
    def din(name, shape, dtype=F32):
        dr[name] = nc.dram_tensor(name, shape, dtype, kind="ExternalInput")

    din("xT", [D, NT]); din("xoT", [D, NT]); din("flag", [128, 1])
    din("consts", [128, 8]); din("idn", [128, 128]); din("r2m", [128, 128])
    din("cos_o", [128, NT]); din("sin_o", [128, NT])
    din("cos_x", [128, NT]); din("sin_x", [128, NT])
    din("wqkvA_T", [D, 3 * D]); din("woutA_T", [D, D])
    din("wqkvB_T", [D, 1536]); din("ln1", [128, NCH]); din("ln2", [128, NCH])
    din("wo_T", [D, D]); din("gate_T", [D, E])
    for b_ in range(8):
        din("w13Gb%d" % b_, [E * 128, 4096])
    din("w2G", [E * 128, 16384])
    din("spc", [128, 64])
    out_t = nc.dram_tensor("out", [D, NT], F32, kind="ExternalOutput")
    h2tm_t = nc.dram_tensor("h2tm", [NT, D], BF16)
    pairflat_t = nc.dram_tensor("pairflat", [NSLOT, 2], I32)
    stg_t = nc.dram_tensor("stg", [NSLOT, D], F32)
    w13B_t = [nc.dram_tensor("w13B%d" % b_, [E * 128, 4096], BF16)
              for b_ in range(8)]
    w2B_t = [nc.dram_tensor("w2B%d" % h_, [E * 128, 8192], BF16)
             for h_ in range(2)]
    dbg_t = {n: nc.dram_tensor("dbg_" + n, shp, F32, kind="ExternalOutput")
             for (n, shp, _dt, _f) in dbg}

    sb = {}
    arena = nc.alloc_sbuf_tensor("arena_reserved", [128, ARENA_SZ], mybir.dt.uint8)
    base = nc.lookup_mloc(arena).addr
    def at(name, shape, dtype, off):
        sb[name] = nc.alloc_sbuf_tensor_at(name, shape, dtype, offset=base + off)
        return sb[name]

    acc = at("acc", [128, NCH * NT], F32, OFF_ACC)
    xn = at("xn", [128, NCH * NT], BF16, OFF_XN)
    areg = at("areg", [128, NCH * NT], BF16, OFF_A)
    pool = at("pool", [128, NT], F32, OFF_POOL)
    misc = at("misc", [128, 64], F32, OFF_MISC)
    miscb = at("miscb", [128, 192], BF16, OFF_MISC + 256)
    r2m_b = at("r2m_b", [128, 128], BF16, OFF_MISC + 640)
    onesRF = at("onesRF", [128, 128], F32, OFF_MISC + 896)
    wbufA = at("wbufA", [128, 6144], BF16, OFF_B)
    woutbufA = at("woutbufA", [128, 4096], BF16, OFF_B)
    s8k = at("s8k", [128, NT], F32, OFF_B + 8192)
    wbufB = at("wbufB", [128, NCH * 512], BF16, OFF_B)
    vfm = at("vfm", [128, NT], BF16, OFF_B + 8192)
    xno_tt = at("xno_tt", [128, TT], BF16, OFF_B + 12288)
    r1bc = at("r1bc", [128, TT], F32, OFF_B + 13312)
    tabs = at("tabs", [128, 2 * NT], BF16, OFF_B + 16384)
    attnB = at("attnBh", [128, 4 * NT], BF16, OFF_B)
    masks = at("masks", [128, 4 * TT], BF16, OFF_B + 16384)
    probs = at("probs", [128, 8 * TT], BF16, OFF_B + 20480)
    sqtt = at("sqtt", [128, TT], F32, OFF_B + 28672)
    avstg = at("avstg", [128, 2 * TT], BF16, OFF_B + 28672)
    rb64 = at("rb64", [128, TT], F32, OFF_B + 30720)
    rb64b = at("rb64b", [128, 2 * TT], BF16, OFF_B + 30720)
    wobuf = at("wobuf", [128, 4 * 1024], BF16, OFF_B + 20480)
    gatebuf = at("gatebuf", [128, NCH * E], BF16, OFF_B + 30720)
    pool2 = at("pool2", [128, NT], F32, OFF_A)
    gw = None
    wf_tm = at("wf_tm", [128, 8], F32, OFF_B + 25088)
    gcol = at("gcol", [128, 16], F32, OFF_B + 25600)
    xslab = at("xslab", [128, 1024], F32, OFF_KV)
    w13buf = at("w13buf", [128, 2 * NCH * 1024], BF16, OFF_B)
    k_own = at("k_own", [128, 2 * NT], BF16, OFF_KV)
    k_oth = at("k_oth", [128, 2 * NT], BF16, OFF_KV + 8192)
    v_tm = at("v_tm", [128, 4 * 32 * 65], BF16, OFF_KV + 16384)
    qcA = at("qcA", [128, 2 * NT], BF16, OFF_KV)
    kcA = at("kcA", [128, 2 * NT], BF16, OFF_KV + 8192)
    vcA = at("vcA", [128, 2 * NT], BF16, OFF_KV + 16384)
    kdA = at("kdA", [128, 2 * 256], BF16, OFF_KV + 24576)
    vdA = at("vdA", [128, 2 * 256], BF16, OFF_KV + 25600)
    vdtA = at("vdtA", [128, 2 * 260], BF16, OFF_KV + 26624)
    credA = at("credA", [128, 32], F32, OFF_KV + 28672)
    w2buf = at("w2buf", [128, 16 * 1024], BF16, OFF_KV)
    gw = at("gw", [128, 16 * TT], BF16, OFF_XN)
    wfT = at("wfT", [8, 16 * 128], BF16, OFF_XN + 16384)
    esel = at("esel", [8, 8 * 128], BF16, OFF_XN + 20480)
    silub = at("silub", [128, 2 * TT], BF16, OFF_XN + 22528)
    idnf = at("idnf", [128, 128], F32, OFF_XN + 24576)
    h2gfm = at("h2gfm", [128, 2 * 4096], BF16, OFF_A)
    stgtm = at("stgtm", [128, 4096], F32, OFF_A + 16384)
    h2g_tm = at("h2g_tm", [128, 2 * 1024], BF16, OFF_XN + 25088)
    spcS = at("spcS", [128, 64], F32, OFF_XN + 29184)
    scolF = at("scolF", [128, 96], F32, OFF_XN + 29440)
    wcolF = at("wcolF", [128, 16], F32, OFF_XN + 29824)
    icol = at("icol", [128, 32], I32, OFF_XN + 29888)
    icol2 = at("icol2", [128, 32], I32, OFF_XN + 30016)
    icol3 = at("icol3", [128, 120], I32, OFF_XN + 30144)
    d2s = at("d2s", [128, 1024], BF16, OFF_XN + 30624)
    icolP = at("icolP", [128, 64], I32, OFF_XN + 25088)

    psB = nc.alloc_psum_tensor("psB", [128, 4096], F32)
    def PB(i, w=512):
        return psB[:, 512 * i:512 * i + w]

    p = Prog(nc)
    stack = contextlib.ExitStack()
    for s_ in p.all_sem_names():
        p.sems[s_] = stack.enter_context(nc.semaphore(s_))

    def T(fn, waits=(), inc=None, sm=False, nc_=False):
        return p.run("tensor", fn, waits, inc, sm, nc_)
    def V(fn, waits=(), inc=None, sm=False, nc_=False):
        return p.run("vector", fn, waits, inc, sm, nc_)
    def Sc(fn, waits=(), inc=None, sm=False, nc_=False):
        return p.run("scalar", fn, waits, inc, sm, nc_)
    def G(fn, waits=(), inc=None, sm=False, nc_=False):
        return p.run("gpsimd", fn, waits, inc, sm, nc_)
    def DG(fn, waits=()): return p.dma("gpsimd", fn, waits)
    def DS(fn, waits=()): return p.dma("sync", fn, waits)
    def DA(fn, waits=()): return p.dma("scalar", fn, waits)

    def phase_barrier():
        toks = {e: p.mark(e) for e in ("tensor", "vector", "scalar", "gpsimd")}
        for e in Prog.ENGINES:
            for e2, t in toks.items():
                if e2 != e and t is not None:
                    p.wait(e, t)

    t_x = DS(lambda g: g.dma_start(
        out=acc[:].rearrange("p (c t) -> p c t", c=NCH),
        in_=dr["xT"].ap().rearrange("(c p) t -> p c t", p=128)))
    t_flag = DS(lambda g: g.dma_start(out=misc[:, 0:1], in_=dr["flag"].ap()))
    t_consts = DS(lambda g: g.dma_start(out=misc[:, 1:9], in_=dr["consts"].ap()))
    t_ln1 = DS(lambda g: g.dma_start(out=misc[:, 16:24], in_=dr["ln1"].ap()))
    t_ln2 = DS(lambda g: g.dma_start(out=misc[:, 24:32], in_=dr["ln2"].ap()))
    t_idn = DG(lambda g: g.dma_start(out=miscb[:, 0:128], in_=dr["idn"].ap()))
    t_r2 = DG(lambda g: g.dma_start(out=r2m_b[:], in_=dr["r2m"].ap()))
    tz0 = G(lambda g: g.memset(s8k[0:16, 0:960], 0.0), inc="G")
    t_zt = DG(lambda g: g.dma_start(
        out=pairflat_t.ap().rearrange("(a b) c -> a (b c)", a=16),
        in_=s8k[0:16, 0:960]), waits=[tz0])
    FLAG = misc[:, 0:1]
    C_EPS = misc[:, 1:2]; C_1EM10 = misc[:, 2:3]; C_M1E9 = misc[:, 3:4]
    LN1c = lambda c: misc[:, 16 + c:17 + c]
    LN2c = lambda c: misc[:, 24 + c:25 + c]
    IDN = miscb[:, 0:128]

    t_ones = V(lambda g: g.memset(misc[:, 9:10], 1.0), sm=True)
    t_onesb = V(lambda g: g.memset(miscb[:, 128:129], 1.0), sm=True)
    t_onesr = V(lambda g: g.memset(onesRF[:], 1.0), sm=True)
    ONES_F = misc[:, 9:10]
    ONES_B = miscb[:, 128:129]

    def bcast_mm(dst_ps, row_ap, base):
        return lambda g: g.matmul(dst_ps, onesRF[base:base + 1, :], row_ap,
                                  start=True, stop=True, tile_position=(base, 0))

    ACCc = lambda c: acc[:, c * NT:(c + 1) * NT]
    XNc = lambda c: xn[:, c * NT:(c + 1) * NT]
    Ac = lambda c: areg[:, c * NT:(c + 1) * NT]
    SQBF = lambda c: s8k[:, c * 256:(c + 1) * 256].bitcast(BF16)

    R_VAR = pool[0:1, :]
    R_A = pool[32:33, :]
    R_B = pool[96:97, :]
    R_R1 = pool[64:65, :]
    R_WF = pool[32:40, :]

    ctx = dict(sb=sb, dr=dr, psB=psB)

    if phases >= 1:
        prev_sum_rd = None
        prev_ssq_rd = None
        prev_sq_rd = None
        for tt in range(NTT):
            asl = lambda c, tt=tt: acc[:, c * NT + tt * TT: c * NT + (tt) * TT + TT]
            rsl = lambda r, tt=tt: r[:, tt * TT:(tt + 1) * TT]
            tok_mm = None
            for c in range(NCH):
                w = [t_x, t_ones]
                if c == 0 and prev_sum_rd is not None:
                    w.append(prev_sum_rd)
                tok_mm = T(lambda g, c=c, asl=asl: g.matmul(
                    PB(0)[0:1, :], ONES_F, asl(c),
                    start=(c == 0), stop=(c == NCH - 1)), waits=w,
                    inc="T" if c == NCH - 1 else None)
            tsq = []
            for c in range(NCH):
                w = [t_x]
                if tt == 0 and c == 0:
                    w += [t_zt]
                if prev_sq_rd is not None:
                    w.append(prev_sq_rd)
                tsq.append(Sc(lambda g, c=c, asl=asl: g.activation(
                    out=SQBF(c), in_=asl(c), func=AF.Square), waits=w, inc="S"))
            tok_mm2 = None
            for c in range(NCH):
                w = [tsq[c], t_onesb]
                if c == 0 and prev_ssq_rd is not None:
                    w.append(prev_ssq_rd)
                tok_mm2 = T(lambda g, c=c: g.matmul(
                    PB(1)[0:1, :], ONES_B, SQBF(c),
                    start=(c == 0), stop=(c == NCH - 1)), waits=w,
                    inc="T" if c == NCH - 1 else None)
            prev_sq_rd = tok_mm2
            T0 = pool[0:1, 0:512]; T1 = pool[0:1, 512:1024]
            T2 = pool[0:1, 1024:1536]; T3 = pool[0:1, 1536:2048]
            V(lambda g: g.tensor_scalar_mul(T0, PB(0)[0:1, :], 1.0 / D),
              waits=[tok_mm], sm=True)
            prev_sum_rd = V(lambda g: g.scalar_tensor_tensor(
                T1, T0, 1.0, T0, op0=ALU.mult, op1=ALU.mult), sm=True)
            prev_ssq_rd = V(lambda g: g.scalar_tensor_tensor(
                T1, PB(1)[0:1, :], 1.0 / D, T1,
                op0=ALU.mult, op1=ALU.subtract), waits=[tok_mm2], sm=True)
            ts1 = Sc(lambda g: g.activation(
                out=T2, in_=T1, func=AF.Sqrt, bias=C_EPS[0:1, :]),
                waits=[prev_ssq_rd, t_consts], sm=True)
            tv4 = V(lambda g: g.reciprocal(T2, T2), waits=[ts1], sm=True)
            V(lambda g, rsl=rsl: g.tensor_copy(rsl(R_A), T2), sm=True)
            V(lambda g, rsl=rsl: g.scalar_tensor_tensor(
                rsl(R_B), T0, -1.0, T2, op0=ALU.mult, op1=ALU.mult), sm=True)
            V(lambda g: g.scalar_tensor_tensor(
                T3, T2, 1.0, T2, op0=ALU.mult, op1=ALU.mult), sm=True)
            tv6 = V(lambda g: g.scalar_tensor_tensor(
                T3, T1, 1.0, T3, op0=ALU.mult, op1=ALU.mult), sm=True)
            ts2 = Sc(lambda g: g.activation(
                out=T3, in_=T3, func=AF.Sqrt, bias=C_EPS[0:1, :]),
                waits=[tv6], sm=True)
            V(lambda g: g.reciprocal(T3, T3), waits=[ts2], sm=True)
            V(lambda g, rsl=rsl: g.tensor_copy(rsl(R_R1), T3), sm=True)
        t_rows = p.mark("vector")

        if sub >= 2:
            tb = None
            for tt in range(NTT):
                tb = T(bcast_mm(psB[:, tt * 512:(tt + 1) * 512],
                                R_A[:, tt * TT:(tt + 1) * TT], 32),
                       waits=[t_rows, t_onesr], inc="T" if tt == NTT - 1 else None)
            xn1 = []
            for c in range(NCH):
                xn1.append(V(lambda g, c=c: g.tensor_tensor(
                    XNc(c), ACCc(c), psB[:, 0:2048], op=ALU.mult), waits=[tb], inc="V"))
        if sub >= 3:
            ta1 = V(lambda g: g.tensor_scalar_add(R_A[:, :], R_A[:, :], 1.0), waits=[tb], inc="V", sm=True)
            tb2 = None
            for tt in range(NTT):
                tb2 = T(bcast_mm(psB[:, 2048 + tt * 512:2048 + (tt + 1) * 512],
                                 R_A[:, tt * TT:(tt + 1) * TT], 32),
                        waits=[ta1], inc="T" if tt == NTT - 1 else None)
            acc1 = []
            for c in range(NCH):
                acc1.append(V(lambda g, c=c: g.tensor_tensor(
                    ACCc(c), ACCc(c), psB[:, 2048:4096], op=ALU.mult),
                    waits=[tb2], inc="V"))
        if sub >= 4:
            tb3 = None
            for tt in range(NTT):
                tb3 = T(bcast_mm(psB[:, tt * 512:(tt + 1) * 512],
                                 R_B[:, tt * TT:(tt + 1) * TT], 96),
                        waits=[xn1[-1]], inc="T" if tt == NTT - 1 else None)
            for c in range(NCH):
                V(lambda g, c=c: g.tensor_tensor(
                    XNc(c), XNc(c), psB[:, 0:2048], op=ALU.add), waits=[tb3])
            for c in range(NCH):
                V(lambda g, c=c: g.tensor_tensor(
                    ACCc(c), ACCc(c), psB[:, 0:2048], op=ALU.add), waits=[tb3, acc1[c]])
        phase_barrier()


    if phases >= 2:
        bank_free = {b: None for b in range(8)}
        probs_free2 = {s_: None for s_ in range(8)}
        avstg_free2 = {0: None, 1: None}
        pend_normA = []
        slot_free = {(par, m): None for par in (0, 1) for m in range(3)}
        wA = dr["wqkvA_T"].ap().rearrange("(i p) o -> p i o", p=128)
        attnA_done = []

        for c in range(NCH):
            par = c % 2
            hA, hB = 2 * c, 2 * c + 1
            wdma = []
            for m in range(3):
                ocol = m * D + c * 128
                slot = wbufA[:, (par * 3 + m) * 1024:(par * 3 + m + 1) * 1024]
                wts = [] if slot_free[(par, m)] is None else [slot_free[(par, m)]]
                wdma.append(DG(lambda g, slot=slot, ocol=ocol: g.dma_start(
                    out=slot.rearrange("p (i o) -> p i o", o=128),
                    in_=wA[:, :, ocol:ocol + 128]), waits=wts))
            dsts = [qcA, kcA, vcA]
            evt = {}
            for m in range(3):
                for tt in range(NTT):
                    j = m * NTT + tt
                    bk = j % 2
                    wts = [wdma[m]]
                    if bank_free[bk] is not None:
                        wts.append(bank_free[bk])
                    tok_mm = None
                    for i in range(NCH):
                        tok_mm = T(lambda g, m=m, i=i, tt=tt, bk=bk, par=par: g.matmul(
                            PB(bk), wbufA[:, (par * 3 + m) * 1024 + i * 128:
                                          (par * 3 + m) * 1024 + (i + 1) * 128],
                            xn[:, i * NT + tt * TT:i * NT + tt * TT + TT],
                            start=(i == 0), stop=(i == NCH - 1)),
                            waits=wts if i == 0 else (),
                            inc="T" if i == NCH - 1 else None)
                    if m == 3:
                        pass
                    slot_free[(par, m)] = tok_mm
                    if m == 0:
                        ev = Sc(lambda g, tt=tt, bk=bk, par=par: g.activation(
                            out=qcA[:, par * NT + tt * TT:par * NT + tt * TT + TT],
                            in_=PB(bk), func=AF.Copy, scale=0.125),
                            waits=[tok_mm], inc="S")
                    else:
                        ev = Sc(lambda g, m=m, tt=tt, bk=bk, par=par: g.activation(
                            out=dsts[m][:, par * NT + tt * TT:par * NT + tt * TT + TT],
                            in_=PB(bk), func=AF.Copy),
                            waits=[tok_mm], inc="S")
                    evt[(m, tt)] = ev
                    bank_free[bk] = ev
            kdp = kdA[:, par * 256:(par + 1) * 256]
            vdp = vdA[:, par * 256:(par + 1) * 256]
            t_kd = V(lambda g, par=par, kdp=kdp: g.tensor_copy(
                kdp, kcA[:, par * NT:(par + 1) * NT].rearrange(
                    "p (n d) -> p n d", d=DIL)[:, :, 0]),
                waits=[evt[(1, NTT - 1)]], sm=True)
            t_vd = V(lambda g, par=par, vdp=vdp: g.tensor_copy(
                vdp, vcA[:, par * NT:(par + 1) * NT].rearrange(
                    "p (n d) -> p n d", d=DIL)[:, :, 0]),
                waits=[evt[(2, NTT - 1)]], sm=True)
            vdtp = vdtA[:, par * 260:(par + 1) * 260]
            t_vdt = []
            for kk in range(2):
                wts = [t_vd, t_idn]
                if bank_free[6] is not None:
                    wts.append(bank_free[6])
                ttk = T(lambda g, kk=kk, vdp=vdp: g.transpose(
                    psB[:, 6 * 512:6 * 512 + 64].bitcast(BF16),
                    vdp[:, kk * 128:(kk + 1) * 128], miscb[:, 0:128]),
                    waits=wts, inc="T")
                for hi in range(2):
                    tc = V(lambda g, hi=hi, kk=kk, vdtp=vdtp: g.tensor_copy(
                        vdtp[:, hi * 130 + kk * 65:hi * 130 + kk * 65 + 64],
                        psB[:, 6 * 512 + hi * 32:6 * 512 + hi * 32 + 32].bitcast(BF16)),
                        waits=[ttk], sm=True)
                    bank_free[6] = tc
                    t_vdt.append(tc)
            t_ones_col = V(lambda g, vdtp=vdtp: g.memset(vdtp[:, 64::65], 1.0), sm=True)
            t_vdt.append(t_ones_col)
            exp_prev = None

            def a2_emit_num(rec, par=par, vdtp=vdtp):
                tt_, etk = rec
                for hi in range(2):
                    nbank = 6 + hi
                    tok_nm = None
                    for kk in range(2):
                        ex, slot = etk[(hi, kk)]
                        wts = [ex] + t_vdt
                        if kk == 0 and bank_free[nbank] is not None:
                            wts.append(bank_free[nbank])
                        tok_nm = T(lambda g, hi=hi, kk=kk, nbank=nbank, slot=slot: g.matmul(
                            PB(nbank)[0:65, :],
                            vdtp[:, hi * 130 + kk * 65:hi * 130 + (kk + 1) * 65],
                            probs[:, slot * TT:(slot + 1) * TT],
                            start=(kk == 0), stop=(kk == 1)),
                            waits=wts, inc="T" if kk == 1 else None)
                        probs_free2[slot] = tok_nm
                    wts = [tok_nm]
                    if avstg_free2[hi] is not None:
                        wts.append(avstg_free2[hi])
                    stok = Sc(lambda g, hi=hi: g.activation(
                        out=avstg[0:65, hi * TT:hi * TT + TT],
                        in_=PB(6 + hi)[0:65, :], func=AF.Copy),
                        waits=wts, inc="S", nc_=True)
                    bank_free[6 + hi] = stok
                    h = hA if hi == 0 else hB
                    pend_normA.append((h, tt_, hi, stok))

            def a2_flush_norm():
                (h, tt_, hi, stok) = pend_normA.pop(0)
                T0r = pool[0:1, 0:512]
                ta = V(lambda g: g.tensor_scalar_add(
                    T0r, avstg[64:65, hi * TT:hi * TT + TT], 1e-10),
                    waits=[stok], sm=True)
                tr = V(lambda g: g.reciprocal(T0r, T0r), sm=True)
                wts = [tr, t_onesr]
                if bank_free[4 + hi] is not None:
                    wts.append(bank_free[4 + hi])
                tbc = T(lambda g, hi=hi: g.matmul(
                    PB(4 + hi)[0:64, :], onesRF[0:1, 0:64], T0r,
                    start=True, stop=True), waits=wts, inc="T")
                tcp = V(lambda g, hi=hi: g.tensor_copy(
                    rb64b[0:64, hi * TT:hi * TT + TT], PB(4 + hi)[0:64, :]),
                    waits=[tbc], inc="V", nc_=True)
                bank_free[4 + hi] = tcp
                tm = V(lambda g, h=h, tt_=tt_, hi=hi: g.tensor_tensor(
                    areg[(h % 2) * 64:(h % 2) * 64 + 64,
                         (h // 2) * NT + tt_ * TT:(h // 2) * NT + tt_ * TT + TT],
                    avstg[0:64, hi * TT:hi * TT + TT],
                    rb64b[0:64, hi * TT:hi * TT + TT], op=ALU.mult),
                    waits=[tcp], inc="V", nc_=True)
                avstg_free2[hi] = tm
                attnA_done.append(tm)

            for tt in range(NTT):
                etk = {}
                for kk in range(2):
                    for hi, r0 in enumerate((0, 64)):
                        sbank = (2 + hi) if (tt * 2 + kk) % 2 == 0 else (4 + hi)
                        slot = (hi * 2 + kk) * 2 + tt % 2
                        wts = [t_kd, evt[(0, NTT - 1)]]
                        if bank_free[sbank] is not None:
                            wts.append(bank_free[sbank])
                        mm = T(lambda g, r0=r0, kk=kk, sbank=sbank, tt=tt, par=par,
                               kdp=kdp: g.matmul(
                            PB(sbank), kdp[r0:r0 + 64, kk * 128:(kk + 1) * 128],
                            qcA[r0:r0 + 64, par * NT + tt * TT:par * NT + tt * TT + TT],
                            start=True, stop=True, tile_position=(r0, 0)),
                            waits=wts, inc="T")
                        cred = credA[:, (par * 2 + hi) * 8:(par * 2 + hi) * 8 + 8]
                        red = V(lambda g, sbank=sbank, cred=cred: g.reduce_max(
                            cred, PB(sbank).rearrange("p (n w) -> p n w", w=BLOCK_M),
                            axis=AX.X), waits=[mm], sm=True)
                        tsub = V(lambda g, sbank=sbank, cred=cred: g.scalar_tensor_tensor(
                            PB(sbank).rearrange("p (n w) -> p n w", w=BLOCK_M),
                            PB(sbank).rearrange("p (n w) -> p n w", w=BLOCK_M),
                            1.0, cred.unsqueeze(2).broadcast_to([128, 8, BLOCK_M]),
                            op0=ALU.mult, op1=ALU.subtract), waits=[red], sm=True)
                        wts2 = [tsub]
                        if probs_free2[slot] is not None:
                            wts2.append(probs_free2[slot])
                        ex = Sc(lambda g, sbank=sbank, slot=slot: g.activation(
                            out=probs[:, slot * TT:(slot + 1) * TT],
                            in_=PB(sbank), func=AF.Exp), waits=wts2, inc="S",
                            nc_=True)
                        etk[(hi, kk)] = (ex, slot)
                        bank_free[sbank] = ex
                while pend_normA:
                    a2_flush_norm()
                if exp_prev is not None:
                    a2_emit_num(exp_prev)
                exp_prev = (tt, etk)
            while pend_normA:
                a2_flush_norm()
            a2_emit_num(exp_prev)
            exp_prev = None
        while pend_normA:
            a2_flush_norm()
        t_attnA = p.mark("vector")
        wO = dr["woutA_T"].ap().rearrange("(i p) o -> p i o", p=128)
        for half in range(2):
            wts = [p.mark("tensor")]
            tw = DG(lambda g, half=half: g.dma_start(
                out=woutbufA[:].rearrange("p (i o) -> p i o", o=1024),
                in_=wO[:, half * 4:(half + 1) * 4, :]), waits=wts)
            for oc in range(NCH):
                for tt in range(NTT):
                    bk = (oc * NTT + tt) % 2
                    wts = [tw, t_attnA]
                    if bank_free[bk] is not None:
                        wts.append(bank_free[bk])
                    tok_mm = None
                    for i in range(4):
                        ic = half * 4 + i
                        tok_mm = T(lambda g, i=i, oc=oc, tt=tt, bk=bk, ic=ic: g.matmul(
                            PB(bk), woutbufA[:, i * 1024 + oc * 128:i * 1024 + (oc + 1) * 128],
                            areg[:, ic * NT + tt * TT:ic * NT + tt * TT + TT],
                            start=(i == 0), stop=(i == 3)),
                            waits=wts if i == 0 else (),
                            inc="T" if i == 3 else None)
                    ev = V(lambda g, oc=oc, tt=tt, bk=bk: g.tensor_add(
                        acc[:, oc * NT + tt * TT:oc * NT + tt * TT + TT],
                        acc[:, oc * NT + tt * TT:oc * NT + tt * TT + TT],
                        PB(bk)), waits=[tok_mm], inc="V")
                    bank_free[bk] = ev
        phase_barrier()


    R2_A = pool2[32:33, :]; R2_B = pool2[96:97, :]; R2_R1 = pool2[0:1, :]
    if phases >= 3:
        xoA = dr["xoT"].ap().rearrange("(c p) t -> p c t", p=128)
        xsl = lambda par: s8k[:, par * 512:(par + 1) * 512]
        sqsl = lambda par: s8k[:, 1024 + par * 256:1024 + (par + 1) * 256].bitcast(BF16)
        prev_sum_rd = None; prev_ssq_rd = None
        prev_mm = {0: None, 1: None}
        prev_sq_mm = {0: None, 1: None}
        for tt in range(NTT):
            tok_mm = None; tok_mm2 = None
            for c in range(NCH):
                par = c % 2
                wts = [] if prev_mm[par] is None else [prev_mm[par]]
                if prev_sq_mm[par] is not None:
                    wts.append(prev_sq_mm[par])
                td = DS(lambda g, c=c, tt=tt, par=par: g.dma_start(
                    out=xsl(par), in_=xoA[:, c, tt * TT:(tt + 1) * TT]), waits=wts)
                w = [td, t_ones]
                if c == 0 and prev_sum_rd is not None:
                    w.append(prev_sum_rd)
                tok_mm = T(lambda g, par=par, c=c: g.matmul(
                    PB(0)[0:1, :], ONES_F, xsl(par),
                    start=(c == 0), stop=(c == NCH - 1)), waits=w,
                    inc="T")
                prev_mm[par] = tok_mm
                tsq = Sc(lambda g, par=par: g.activation(
                    out=sqsl(par), in_=xsl(par), func=AF.Square),
                    waits=[td] + ([prev_sq_mm[par]] if prev_sq_mm[par] else []),
                    inc="S")
                w = [tsq, t_onesb]
                if c == 0 and prev_ssq_rd is not None:
                    w.append(prev_ssq_rd)
                tok_mm2 = T(lambda g, par=par, c=c: g.matmul(
                    PB(1)[0:1, :], ONES_B, sqsl(par),
                    start=(c == 0), stop=(c == NCH - 1)), waits=w,
                    inc="T")
                prev_sq_mm[par] = tok_mm2
            rsl = lambda r, tt=tt: r[:, tt * TT:(tt + 1) * TT]
            T0 = pool[0:1, 0:512]; T1 = pool[0:1, 512:1024]
            T2 = pool[0:1, 1024:1536]
            V(lambda g: g.tensor_scalar_mul(T0, PB(0)[0:1, :], 1.0 / D),
              waits=[tok_mm], sm=True)
            prev_sum_rd = V(lambda g: g.scalar_tensor_tensor(
                T1, T0, 1.0, T0, op0=ALU.mult, op1=ALU.mult), sm=True)
            prev_ssq_rd = V(lambda g: g.scalar_tensor_tensor(
                T1, PB(1)[0:1, :], 1.0 / D, T1,
                op0=ALU.mult, op1=ALU.subtract), waits=[tok_mm2], sm=True)
            tsq1 = Sc(lambda g: g.activation(out=T2, in_=T1, func=AF.Sqrt,
                      bias=C_EPS[0:1, :]), waits=[prev_ssq_rd, t_consts], sm=True)
            V(lambda g: g.reciprocal(T2, T2), waits=[tsq1], sm=True)
            V(lambda g, rsl=rsl: g.tensor_copy(rsl(R2_A), T2), sm=True)
            V(lambda g, rsl=rsl: g.scalar_tensor_tensor(
                rsl(R2_B), T0, -1.0, T2, op0=ALU.mult, op1=ALU.mult), sm=True)
            V(lambda g: g.scalar_tensor_tensor(
                T2, T2, 1.0, T2, op0=ALU.mult, op1=ALU.mult), sm=True)
            tva = V(lambda g: g.scalar_tensor_tensor(
                T2, T1, 1.0, T2, op0=ALU.mult, op1=ALU.mult), sm=True)
            tsq2 = Sc(lambda g: g.activation(out=T2, in_=T2, func=AF.Sqrt,
                      bias=C_EPS[0:1, :]), waits=[tva], sm=True)
            V(lambda g: g.reciprocal(T2, T2), waits=[tsq2], sm=True)
            V(lambda g, rsl=rsl: g.tensor_copy(rsl(R2_R1), T2), sm=True)
        phase_barrier()

    if phases >= 4:
        bank_free = {b: None for b in range(8)}
        wB = dr["wqkvB_T"].ap().rearrange("(i p) o -> p i o", p=128)

        def scale_tables(r1_row, r1_base):
            toks = []
            for tt in range(NTT):
                wts = [t_onesr]
                if bank_free[7] is not None:
                    wts.append(bank_free[7])
                tb = T(lambda g, tt=tt: g.matmul(
                    PB(7), onesRF[r1_base:r1_base + 1, :],
                    r1_row[:, tt * TT:(tt + 1) * TT], start=True, stop=True,
                    tile_position=(r1_base, 0)), waits=wts, inc="T")
                tm = None
                for half2 in range(2):
                    tm = V(lambda g, tt=tt, half2=half2: g.tensor_tensor(
                        tabs[:, half2 * NT + tt * TT:half2 * NT + (tt + 1) * TT],
                        tabs[:, half2 * NT + tt * TT:half2 * NT + (tt + 1) * TT],
                        PB(7), op=ALU.mult), waits=[tb], inc="V")
                bank_free[7] = tm
                toks.append(tm)
            return toks

        def rope_chunk(dst, dst_off, tok_plain):
            toks = []
            for tt in range(NTT):
                wts = [tok_plain, t_r2]
                if bank_free[6] is not None:
                    wts.append(bank_free[6])
                tr = T(lambda g, tt=tt: g.matmul(
                    PB(6), r2m_b[:], dst[:, dst_off + tt * TT:dst_off + (tt + 1) * TT],
                    start=True, stop=True), waits=wts, inc="T")
                t1 = V(lambda g, tt=tt: g.scalar_tensor_tensor(
                    xno_tt[:], PB(6), 1.0,
                    tabs[:, NT + tt * TT:NT + (tt + 1) * TT],
                    op0=ALU.mult, op1=ALU.mult), waits=[tr], inc="V")
                bank_free[6] = t1
                t2 = V(lambda g, tt=tt: g.tensor_tensor(
                    dst[:, dst_off + tt * TT:dst_off + (tt + 1) * TT],
                    dst[:, dst_off + tt * TT:dst_off + (tt + 1) * TT],
                    tabs[:, tt * TT:(tt + 1) * TT], op=ALU.mult), waits=[t1], inc="V")
                t3 = V(lambda g, tt=tt: g.tensor_tensor(
                    dst[:, dst_off + tt * TT:dst_off + (tt + 1) * TT],
                    dst[:, dst_off + tt * TT:dst_off + (tt + 1) * TT],
                    xno_tt[:], op=ALU.add), waits=[t2], inc="V")
                toks.append(t3)
            return toks

        vt_done = []

        def v_transposes(v_psum_bank, tt, oc, half_own, r1bc_tok, mm_tok):
            ev = V(lambda g, v_psum_bank=v_psum_bank: g.scalar_tensor_tensor(
                vfm[:, 0:512], PB(v_psum_bank), 1.0, r1bc[:, 0:512],
                op0=ALU.mult, op1=ALU.mult), waits=[r1bc_tok, mm_tok], inc="V")
            for j in range(4):
                K = half_own * 16 + tt * 4 + j
                wts = [ev, t_idn]
                if bank_free[5] is not None:
                    wts.append(bank_free[5])
                ttk = T(lambda g, j=j: g.transpose(
                    psB[:, 5 * 512:5 * 512 + 64].bitcast(BF16),
                    vfm[:, j * 128:(j + 1) * 128], miscb[:, 0:128]),
                    waits=wts, inc="T")
                for hi in range(2):
                    kvh = 2 * oc + hi
                    tc = V(lambda g, hi=hi, kvh=kvh, K=K: g.tensor_copy(
                        v_tm[:, kvh * 2080 + K * 65:kvh * 2080 + K * 65 + 64],
                        psB[:, 5 * 512 + hi * 32:5 * 512 + hi * 32 + 32].bitcast(BF16)),
                        waits=[ttk], sm=True)
                    bank_free[5] = tc
                    vt_done.append(tc)
            return ev

        t_vones = []
        for kvh in range(NKV):
            t_vones.append(V(lambda g, kvh=kvh: g.memset(
                v_tm[:, kvh * 2080 + 64:(kvh + 1) * 2080:65], 1.0), sm=True))

        t_wkv = DG(lambda g: g.dma_start(
            out=wbufB[:].rearrange("p (i o) -> p i o", o=512),
            in_=wB[:, :, 1024:1536]))
        t_wsc = []
        for i in range(NCH):
            t_wsc.append(V(lambda g, i=i: g.tensor_scalar_mul(
                wbufB[:, i * 512:(i + 1) * 512], wbufB[:, i * 512:(i + 1) * 512],
                LN1c(i)), waits=[t_wkv, t_ln1], inc="V"))

        t_tabs = [DG(lambda g: g.dma_start(out=tabs[:, 0:NT], in_=dr["cos_x"].ap())),
                  DG(lambda g: g.dma_start(out=tabs[:, NT:2 * NT], in_=dr["sin_x"].ap()))]
        p.wait("vector", t_tabs[0]); p.wait("vector", t_tabs[1])
        scale_tables(R2_R1, 0)

        prev_dma = {0: None, 1: None}
        kv_evac = {}
        for tt in range(NTT):
            wts = [t_onesr]
            if bank_free[4] is not None:
                wts.append(bank_free[4])
            tba = T(lambda g, tt=tt: g.matmul(
                PB(4), onesRF[32:33, :], R2_A[:, tt * TT:(tt + 1) * TT],
                start=True, stop=True, tile_position=(32, 0)), waits=wts, inc="T")
            wts = []
            if bank_free[6] is not None:
                wts.append(bank_free[6])
            tbb = T(lambda g, tt=tt: g.matmul(
                PB(6), onesRF[96:97, :], R2_B[:, tt * TT:(tt + 1) * TT],
                start=True, stop=True, tile_position=(96, 0)), waits=wts, inc="T")
            toks_mm = {oc: None for oc in range(4)}
            tok_prev_kv = None
            xsl4 = lambda par: xslab[:, par * 512:(par + 1) * 512]
            for c in range(NCH):
                par = c % 2
                wts = [] if prev_dma[par] is None else [prev_dma[par]]
                td = DS(lambda g, c=c, tt=tt, par=par: g.dma_start(
                    out=xsl4(par), in_=xoA[:, c, tt * TT:(tt + 1) * TT]), waits=wts)
                wts1 = [td, tba]
                if tok_prev_kv is not None:
                    wts1.append(tok_prev_kv)
                tx1 = V(lambda g, par=par: g.tensor_tensor(
                    xno_tt[:], xsl4(par), PB(4), op=ALU.mult),
                    waits=wts1, inc="V")
                tx2 = V(lambda g: g.tensor_tensor(
                    xno_tt[:], xno_tt[:], PB(6), op=ALU.add),
                    waits=[tbb], inc="V")
                prev_dma[par] = tx1
                for oc in range(4):
                    wts = [tx2, t_wsc[c]]
                    if c == 0 and bank_free[oc] is not None:
                        wts.append(bank_free[oc])
                    toks_mm[oc] = T(lambda g, oc=oc, c=c: g.matmul(
                        PB(oc), wbufB[:, c * 512 + oc * 128:c * 512 + (oc + 1) * 128],
                        xno_tt[:], start=(c == 0), stop=(c == NCH - 1)),
                        waits=wts, inc="T")
                tok_prev_kv = toks_mm[3]
            bank_free[4] = toks_mm[0]
            bank_free[6] = toks_mm[0]
            for oc in range(2):
                ev = Sc(lambda g, oc=oc, tt=tt: g.activation(
                    out=k_oth[:, oc * NT + tt * TT:oc * NT + (tt + 1) * TT],
                    in_=PB(oc), func=AF.Copy), waits=[toks_mm[oc]], inc="S")
                kv_evac[(oc, tt)] = ev
                bank_free[oc] = ev
            wts = [t_onesr]
            if bank_free[7] is not None:
                wts.append(bank_free[7])
            tbr = T(lambda g, tt=tt: g.matmul(
                PB(7), onesRF[0:1, :], R2_R1[:, tt * TT:(tt + 1) * TT],
                start=True, stop=True), waits=wts, inc="T")
            tcr = V(lambda g: g.tensor_copy(r1bc[:, 0:512], PB(7)), waits=[tbr], inc="V")
            bank_free[7] = tcr
            for oc in range(2):
                ev = v_transposes(2 + oc, tt, oc, 0, tcr, toks_mm[2 + oc])
                bank_free[2 + oc] = ev
        for oc in range(2):
            rope_chunk(k_oth, oc * NT, kv_evac[(oc, NTT - 1)])
        t_rope_oth = p.mark("vector")

        t_tabs2 = [DG(lambda g: g.dma_start(out=tabs[:, 0:NT], in_=dr["cos_o"].ap()),
                      waits=[t_rope_oth]),
                   DG(lambda g: g.dma_start(out=tabs[:, NT:2 * NT], in_=dr["sin_o"].ap()),
                      waits=[t_rope_oth])]
        p.wait("vector", t_tabs2[0]); p.wait("vector", t_tabs2[1])
        scale_tables(R_R1, 64)
        kv_evac2 = {}
        for tt in range(NTT):
            wts = [t_onesr]
            if bank_free[7] is not None:
                wts.append(bank_free[7])
            tbr = T(lambda g, tt=tt: g.matmul(
                PB(7), onesRF[64:65, :], R_R1[:, tt * TT:(tt + 1) * TT],
                start=True, stop=True, tile_position=(64, 0)), waits=wts, inc="T")
            tcr = V(lambda g: g.tensor_copy(r1bc[:, 0:512], PB(7)), waits=[tbr], inc="V")
            bank_free[7] = tcr
            for oc in range(4):
                wts = []
                if bank_free[oc] is not None:
                    wts.append(bank_free[oc])
                tok_mm = None
                for i in range(NCH):
                    tok_mm = T(lambda g, oc=oc, i=i, tt=tt: g.matmul(
                        PB(oc), wbufB[:, i * 512 + oc * 128:i * 512 + (oc + 1) * 128],
                        xn[:, i * NT + tt * TT:i * NT + tt * TT + TT],
                        start=(i == 0), stop=(i == NCH - 1)),
                        waits=(wts + [t_wsc[i]]) if i == 0 else [t_wsc[i]],
                        inc="T" if i == NCH - 1 else None)
                if oc < 2:
                    ev = Sc(lambda g, oc=oc, tt=tt: g.activation(
                        out=k_own[:, oc * NT + tt * TT:oc * NT + (tt + 1) * TT],
                        in_=PB(oc), func=AF.Copy), waits=[tok_mm], inc="S")
                    kv_evac2[(oc, tt)] = ev
                    bank_free[oc] = ev
                else:
                    ev = v_transposes(oc, tt, oc - 2, 1, tcr, tok_mm)
                    bank_free[oc] = ev
        for oc in range(2):
            rope_chunk(k_own, oc * NT, kv_evac2[(oc, NTT - 1)])

        t_eA = V(lambda g: g.memset(misc[:, 10:11], 0.0), sm=True)
        t_eA2 = V(lambda g: g.memset(misc[0:64, 10:11], 1.0), sm=True)
        t_eB = V(lambda g: g.memset(misc[:, 11:12], 0.0), sm=True)
        t_eB2 = V(lambda g: g.memset(misc[64:128, 11:12], 1.0), sm=True)
        t_qn0 = V(lambda g: g.memset(gcol[0:1, :], 0.0), sm=True)
        t_kn0 = V(lambda g: g.memset(wf_tm[0:1, :], 0.0), sm=True)

        sqprev = [None]

        def sq_norm_max(src_ap, dst_col, e_col, bank, wts_extra=()):
            w0 = list(wts_extra)
            if sqprev[0] is not None:
                w0.append(sqprev[0])
            tsqq = Sc(lambda g: g.activation(out=sqtt[:], in_=src_ap,
                                             func=AF.Square),
                      waits=w0, inc="S")
            wts = [tsqq, t_eA2, t_eB2]
            if bank_free[bank] is not None:
                wts.append(bank_free[bank])
            tmm = T(lambda g: g.matmul(PB(bank)[0:1, :], e_col, sqtt[:],
                                       start=True, stop=True), waits=wts, inc="T")
            sqprev[0] = tmm
            tred = V(lambda g: g.reduce_max(pool[0:1, 0:1], PB(bank)[0:1, :],
                                            axis=AX.X), waits=[tmm], sm=True)
            bank_free[bank] = tred
            tmax = V(lambda g: g.tensor_max(dst_col, dst_col, pool[0:1, 0:1]),
                     waits=[t_qn0, t_kn0], sm=True)
            return tmax

        t_rope_all = p.mark("vector")
        kn_toks = []
        for reg, regname in ((k_oth, 0), (k_own, 1)):
            for oc in range(2):
                for tt in range(NTT):
                    for hi, ec in ((0, misc[:, 10:11]), (1, misc[:, 11:12])):
                        kvh = 2 * oc + hi
                        kn_toks.append(sq_norm_max(
                            reg[:, oc * NT + tt * TT:oc * NT + (tt + 1) * TT],
                            wf_tm[0:1, kvh:kvh + 1], ec, 6, [t_rope_all]))

        q_toks = {}
        for grp in range(2):
            wts = [p.mark("tensor")]
            twq = DG(lambda g, grp=grp: g.dma_start(
                out=wbufB[:].rearrange("p (i o) -> p i o", o=512),
                in_=wB[:, :, grp * 512:(grp + 1) * 512]), waits=wts)
            t_wscq = []
            for i in range(NCH):
                t_wscq.append(V(lambda g, i=i: g.tensor_scalar_mul(
                    wbufB[:, i * 512:(i + 1) * 512],
                    wbufB[:, i * 512:(i + 1) * 512], LN1c(i)),
                    waits=[twq, t_ln1], inc="V"))
            for lc in range(4):
                cj = grp * 4 + lc
                for tt in range(NTT):
                    bk = (lc * NTT + tt) % 2
                    wts = []
                    if bank_free[bk] is not None:
                        wts.append(bank_free[bk])
                    tok_mm = None
                    for i in range(NCH):
                        tok_mm = T(lambda g, lc=lc, i=i, tt=tt, bk=bk: g.matmul(
                            PB(bk), wbufB[:, i * 512 + lc * 128:i * 512 + (lc + 1) * 128],
                            xn[:, i * NT + tt * TT:i * NT + tt * TT + TT],
                            start=(i == 0), stop=(i == NCH - 1)),
                            waits=(wts + [t_wscq[i]]) if i == 0 else [t_wscq[i]],
                            inc="T" if i == NCH - 1 else None)
                    ev = Sc(lambda g, cj=cj, tt=tt, bk=bk: g.activation(
                        out=areg[:, cj * NT + tt * TT:cj * NT + (tt + 1) * TT],
                        in_=PB(bk), func=AF.Copy), waits=[tok_mm], inc="S")
                    bank_free[bk] = ev
                    q_toks[(cj, tt)] = ev
                rtoks = rope_chunk(areg, cj * NT, q_toks[(cj, NTT - 1)])
                hA = 8 * (cj // 4) + (cj % 4)
                for hi, ec in ((0, misc[:, 10:11]), (1, misc[:, 11:12])):
                    h = hA + 4 * hi
                    for tt in range(NTT):
                        sq_norm_max(areg[:, cj * NT + tt * TT:cj * NT + (tt + 1) * TT],
                                    gcol[0:1, h:h + 1], ec, 6, [rtoks[tt]])
        for h in range(NH):
            kvh = h // 4
            tq = V(lambda g, h=h, kvh=kvh: g.tensor_tensor(
                pool[0:1, 0:1], gcol[0:1, h:h + 1], wf_tm[0:1, kvh:kvh + 1],
                op=ALU.mult), sm=True)
            ts_ = Sc(lambda g: g.activation(out=pool[0:1, 0:1], in_=pool[0:1, 0:1],
                                            func=AF.Sqrt), waits=[tq], sm=True)
            tsc = V(lambda g: g.tensor_scalar_mul(pool[0:1, 0:1], pool[0:1, 0:1],
                                                  -SCALE), waits=[ts_], sm=True)
            wts = [tsc, t_onesr]
            if bank_free[6] is not None:
                wts.append(bank_free[6])
            tbc = T(lambda g: g.matmul(PB(6)[:, 0:1], onesRF[0:1, :],
                                       pool[0:1, 0:1], start=True, stop=True),
                    waits=wts, inc="T")
            tcp = V(lambda g, h=h: g.tensor_copy(misc[:, 32 + h:33 + h], PB(6)[:, 0:1]),
                    waits=[tbc], sm=True)
            bank_free[6] = tcp
            V(lambda g, h=h: g.scalar_tensor_tensor(
                misc[:, 48 + h:49 + h], FLAG, -1e9, misc[:, 32 + h:33 + h],
                op0=ALU.mult, op1=ALU.add), waits=[t_flag], sm=True)
        t_masks = []
        t_tabs_done = [p.mark("vector"), p.mark("tensor")]
        for jj in range(4):
            tm0 = G(lambda g, jj=jj: g.memset(masks[:, jj * TT:(jj + 1) * TT], 1.0),
                    waits=t_tabs_done, inc="G")
            t_masks.append(G(lambda g, jj=jj: g.affine_select(
                out=masks[:, jj * TT:(jj + 1) * TT],
                in_=masks[:, jj * TT:(jj + 1) * TT],
                compare_op=ALU.is_ge, fill=0.0, base=-128 * jj,
                pattern=[[1, TT]], channel_multiplier=-1), waits=[tm0], inc="G"))
        phase_barrier()

    if phases >= 4.7:
        wcast_toks = {}
        bank_free = {b: None for b in range(8)}
        probs_free = {0: None, 1: None}
        wO5 = dr["wo_T"].ap().rearrange("(i p) o -> p i o", p=128)
        PH5_LIM = int(_os.environ.get("PH5_LIM", "32"))
        attnB_war = None
        probs_free = {s_: None for s_ in range(4)}
        av_free = {0: None, 1: None}
        avstg_free = {0: None, 1: None}
        pending_norm = []

        def emit_norm():
            (j_, qt_, stok) = pending_norm.pop(0)
            for hi in range(2):
                T0r = pool[0:1, 0:512]
                tr = V(lambda g_, hi=hi: g_.reciprocal(
                    T0r, avstg[64:65, hi * TT:hi * TT + TT]),
                    waits=[stok[hi]], sm=True)
                wts = [tr, t_onesr]
                if bank_free[4 + hi] is not None:
                    wts.append(bank_free[4 + hi])
                tbc = T(lambda g_, hi=hi: g_.matmul(
                    PB(4 + hi)[0:64, :], onesRF[0:1, 0:64], T0r,
                    start=True, stop=True), waits=wts, inc="T")
                tcp = V(lambda g_, hi=hi: g_.tensor_copy(
                    rb64b[0:64, hi * TT:hi * TT + TT], PB(4 + hi)[0:64, :]),
                    waits=[tbc], inc="V", nc_=True)
                bank_free[4 + hi] = tcp
                tmul = V(lambda g_, hi=hi, j_=j_, qt_=qt_: g_.tensor_tensor(
                    attnB[hi * 64:hi * 64 + 64,
                          j_ * NT + qt_ * TT:j_ * NT + (qt_ + 1) * TT],
                    avstg[0:64, hi * TT:hi * TT + TT],
                    rb64b[0:64, hi * TT:hi * TT + TT], op=ALU.mult),
                    waits=[tcp] + ([attnB_war] if attnB_war else []),
                    inc="V", nc_=True)
                avstg_free[hi] = tmul

        for gh in range(2):
            kvA, kvB = 2 * gh, 2 * gh + 1
            for j in range(4):
                cj = 4 * gh + j
                hA = 8 * gh + j
                hB = hA + 4
                for qt in range(NTT):
                    if gh * 16 + j * 4 + qt >= PH5_LIM:
                        continue
                    kchunks = ([(0, kk) for kk in range(16)] +
                               [(1, kk) for kk in range(4 * qt + 4)])
                    groups = []
                    for own in (0, 1):
                        lst = [kc for kc in kchunks if kc[0] == own]
                        groups += [lst[i:i + 2] for i in range(0, len(lst), 2)]
                    ng = len(groups)
                    ginfo = []
                    av_last = {}

                    def emit_av(gi_rec, first, qt=qt, kvA=kvA, kvB=kvB):
                        etoks, sl, grp = gi_rec
                        for hi in range(2):
                            kvh = kvA if hi == 0 else kvB
                            avbank = 6 + hi
                            slot = hi * 2 + sl
                            for idx, (own_, kk) in enumerate(grp):
                                K_ = own_ * 16 + kk
                                wts = [etoks[hi]]
                                if first and idx == 0 and av_free[hi] is not None:
                                    wts.append(av_free[hi])
                                islast = (own_ == 1 and kk == 4 * qt + 3)
                                tokv = T(lambda g_, kvh=kvh, K_=K_, slot=slot, idx=idx,
                                         avbank=avbank, st=(first and idx == 0),
                                         sp=islast: g_.matmul(
                                    PB(avbank)[0:65, :],
                                    v_tm[:, kvh * 2080 + K_ * 65:kvh * 2080 + (K_ + 1) * 65],
                                    probs[:, slot * 1024 + idx * TT:slot * 1024 + (idx + 1) * TT],
                                    start=st, stop=sp), waits=wts, inc="T")
                                probs_free[slot] = tokv
                                if islast:
                                    av_last[hi] = tokv
                    for gi, grp in enumerate(groups):
                        own = grp[0][0]
                        n = len(grp)
                        ksrc = k_own if own else k_oth
                        sl = gi % 2
                        sc_toks = {}
                        for idx, (own_, kk) in enumerate(grp):
                            for hi, (r0, b0) in enumerate(((0, 0), (64, 2))):
                                wts = []
                                if bank_free[b0 + idx] is not None:
                                    wts.append(bank_free[b0 + idx])
                                sc_toks[(hi, idx)] = T(
                                    lambda g_, r0=r0, b0=b0, idx=idx, kk=kk, gh=gh,
                                           ksrc=ksrc, cj=cj, qt=qt: g_.matmul(
                                        PB(b0 + idx),
                                        ksrc[r0:r0 + 64, gh * NT + kk * 128:gh * NT + kk * 128 + 128],
                                        areg[r0:r0 + 64, cj * NT + qt * TT:cj * NT + (qt + 1) * TT],
                                        start=True, stop=True, tile_position=(r0, 0)),
                                    waits=wts, inc="T")
                        if gi >= 1:
                            emit_av(ginfo[gi - 1], first=(gi == 1))
                        etoks = {}
                        for hi in range(2):
                            h = hA if hi == 0 else hB
                            b0 = 0 if hi == 0 else 2
                            slot = hi * 2 + sl
                            bias_col = misc[:, (32 if own else 48) + h:
                                            (32 if own else 48) + h + 1]
                            wts = [sc_toks[(hi, n - 1)], sc_toks[(hi, 0)]]
                            if probs_free[slot] is not None:
                                wts.append(probs_free[slot])
                            ex = Sc(lambda g_, b0=b0, n=n, slot=slot,
                                    bias_col=bias_col: g_.activation(
                                out=probs[:, slot * 1024:slot * 1024 + n * TT],
                                in_=psB[:, b0 * 512:(b0 + n) * 512],
                                func=AF.Exp, bias=bias_col, scale=SCALE),
                                waits=wts, inc="S", nc_=True)
                            for idx in range(n):
                                bank_free[b0 + idx] = ex
                            mtok = ex
                            if own:
                                for idx, (_, kk) in enumerate(grp):
                                    jj = kk - 4 * qt
                                    if 0 <= jj < 4:
                                        mtok = V(lambda g_, slot=slot, idx=idx, jj=jj: g_.tensor_tensor(
                                            probs[:, slot * 1024 + idx * TT:slot * 1024 + (idx + 1) * TT],
                                            probs[:, slot * 1024 + idx * TT:slot * 1024 + (idx + 1) * TT],
                                            masks[:, jj * TT:(jj + 1) * TT],
                                            op=ALU.mult), waits=[ex, t_masks[jj]],
                                            inc="V", nc_=True)
                            etoks[hi] = mtok
                        ginfo.append((etoks, sl, grp))

                    if pending_norm:
                        emit_norm()
                    emit_av(ginfo[ng - 1], first=(ng == 1))
                    stok = {}
                    for hi in range(2):
                        wts = [av_last[hi]]
                        if avstg_free[hi] is not None:
                            wts.append(avstg_free[hi])
                        stok[hi] = Sc(lambda g_, hi=hi: g_.activation(
                            out=avstg[0:65, hi * TT:hi * TT + TT],
                            in_=PB(6 + hi)[0:65, :], func=AF.Copy),
                            waits=wts, inc="S", nc_=True)
                        av_free[hi] = stok[hi]
                    pending_norm.append((j, qt, stok))
            while pending_norm:
                emit_norm()
            if phases < 5:
                continue
            t_attnB = p.mark("vector")
            wts = [p.mark("tensor"), p.mark("scalar")]
            tw = DG(lambda g_, gh=gh: g_.dma_start(
                out=wobuf[:].rearrange("p (i o) -> p i o", o=1024),
                in_=wO5[:, 4 * gh:4 * gh + 4, :]), waits=wts)
            if gh == 0 and SPARSE and phases >= 8:
                for b_ in range(8):
                    wcast_toks[b_] = DG(lambda g_, b_=b_: g_.dma_start(
                        out=w13B_t[b_].ap(), in_=dr["w13Gb%d" % b_].ap()))
                for h_ in range(2):
                    wcast_toks["w2%d" % h_] = DG(lambda g_, h_=h_: g_.dma_start(
                        out=w2B_t[h_].ap(),
                        in_=dr["w2G"].ap()[:, h_ * 8192:(h_ + 1) * 8192]))
            for oc in range(NCH):
                for tt in range(NTT):
                    bk = (oc * NTT + tt) % 2
                    wts = [tw, t_attnB]
                    if bank_free[bk] is not None:
                        wts.append(bank_free[bk])
                    tok_mm = None
                    for i in range(4):
                        tok_mm = T(lambda g_, i=i, oc=oc, tt=tt, bk=bk: g_.matmul(
                            PB(bk), wobuf[:, i * 1024 + oc * 128:i * 1024 + (oc + 1) * 128],
                            attnB[:, i * NT + tt * TT:i * NT + tt * TT + TT],
                            start=(i == 0), stop=(i == 3)),
                            waits=wts if i == 0 else (),
                            inc="T" if i == 3 else None)
                    ev1 = V(lambda g_, oc=oc, tt=tt, bk=bk: g_.tensor_add(
                        acc[:, oc * NT + tt * TT:oc * NT + tt * TT + TT],
                        acc[:, oc * NT + tt * TT:oc * NT + tt * TT + TT],
                        PB(bk)), waits=[tok_mm], inc="V", nc_=True)
                    ev2 = V(lambda g_, oc=oc, tt=tt, bk=bk: g_.tensor_add(
                        xn[:, oc * NT + tt * TT:oc * NT + tt * TT + TT],
                        xn[:, oc * NT + tt * TT:oc * NT + tt * TT + TT],
                        PB(bk)), waits=[tok_mm], inc="V", nc_=True)
                    bank_free[bk] = ev2
            tmk = p.mark("tensor")
            probs_free = {s_: tmk for s_ in range(4)}
            attnB_war = tmk
        phase_barrier()

    if phases >= 6:
        bank_free = {b: None for b in range(8)}
        t_gw = DG(lambda g: g.dma_start(
            out=gatebuf[:].rearrange("p (c e) -> p c e", e=E),
            in_=dr["gate_T"].ap().rearrange("(c p) e -> p c e", p=128)))
        R_R2 = pool[32:33, :]
        prev_ssq_rd = None
        prev_sq_rd = None
        for tt in range(NTT):
            tok_mm2 = None
            for c in range(NCH):
                w = [t_onesb]
                if prev_sq_rd is not None:
                    w.append(prev_sq_rd)
                tsq = Sc(lambda g, c=c, tt=tt: g.activation(
                    out=SQBF(c), in_=xn[:, c * NT + tt * TT:c * NT + tt * TT + TT],
                    func=AF.Square), waits=w)
                w2_ = [tsq]
                if c == 0 and prev_ssq_rd is not None:
                    w2_.append(prev_ssq_rd)
                tok_mm2 = T(lambda g, c=c: g.matmul(
                    PB(4)[0:1, :], ONES_B, SQBF(c),
                    start=(c == 0), stop=(c == NCH - 1)), waits=w2_,
                    inc="T")
            prev_sq_rd = tok_mm2
            T2 = pool[0:1, 1024:1536]
            w3_ = [tok_mm2, t_consts]
            if prev_ssq_rd is not None:
                w3_.append(prev_ssq_rd)
            ts1 = Sc(lambda g: g.activation(
                out=T2, in_=PB(4)[0:1, :], func=AF.Sqrt, bias=C_EPS[0:1, :],
                scale=1.0 / D), waits=w3_, sm=True)
            V(lambda g: g.reciprocal(T2, T2), waits=[ts1], sm=True)
            prev_ssq_rd = V(lambda g, tt=tt: g.tensor_copy(
                R_R2[:, tt * TT:(tt + 1) * TT], T2), sm=True)
        t_r2 = p.mark("vector")
        tb = None
        for tt in range(NTT):
            tb = T(lambda g, tt=tt: g.matmul(
                psB[:, tt * 512:(tt + 1) * 512], onesRF[32:33, :],
                R_R2[:, tt * TT:(tt + 1) * TT], start=True, stop=True,
                tile_position=(32, 0)), waits=[t_r2, t_onesr], inc="T")
        h2t = []
        for c in range(NCH):
            h2t.append(V(lambda g, c=c: g.scalar_tensor_tensor(
                areg[:, c * NT:(c + 1) * NT], xn[:, c * NT:(c + 1) * NT],
                LN2c(c), psB[:, 0:2048], op0=ALU.mult, op1=ALU.mult),
                waits=[tb, t_ln2], inc="V"))
        t_h2 = p.mark("vector")
        if SPARSE and phases >= 8:
            s0_bank = {0: None, 1: None}
            last_sil_rd = None
            h2tm_dmas = []
            for g16 in range(16):
                bk0 = g16 % 2
                tkl = None
                for c in range(NCH):
                    wts = [t_idn, t_h2]
                    if c == 0 and s0_bank[bk0] is not None:
                        wts.append(s0_bank[bk0])
                    tkl = T(lambda g, c=c, g16=g16, bk0=bk0: g.transpose(
                        psB[:, bk0 * 512 + c * 64:bk0 * 512 + (c + 1) * 64].bitcast(BF16),
                        areg[:, c * NT + g16 * 128:c * NT + (g16 + 1) * 128],
                        IDN), waits=wts, inc="T" if c == NCH - 1 else None)
                wts2 = [tkl]
                if last_sil_rd is not None:
                    wts2.append(last_sil_rd)
                ev0 = Sc(lambda g, bk0=bk0: g.activation(
                    out=silub[:], in_=psB[:, bk0 * 512:(bk0 + 1) * 512].bitcast(BF16),
                    func=AF.Copy), waits=wts2, inc="S", nc_=True)
                s0_bank[bk0] = ev0
                d0 = DS(lambda g, g16=g16: g.dma_start(
                    out=h2tm_t.ap()[g16 * 128:(g16 + 1) * 128, :], in_=silub[:]),
                    waits=[ev0])
                last_sil_rd = d0
                h2tm_dmas.append(d0)
            t_s0_T = p.mark("tensor")
        t_idnf = DS(lambda g: g.dma_start(out=idnf[:], in_=dr["idn"].ap()),
                    waits=[t_h2])
        t_esel = []
        for e in range(E):
            tz = G(lambda g, e=e: g.memset(esel[:, e * 128:(e + 1) * 128], 0.0),
                   waits=[t_h2])
            t_esel.append(G(lambda g, e=e: g.affine_select(
                out=esel[:, e * 128:(e + 1) * 128],
                in_=esel[:, e * 128:(e + 1) * 128],
                compare_op=ALU.not_equal, fill=1.0, base=-e,
                pattern=[[0, 128]], channel_multiplier=1), waits=[tz]))
        tok_mm = None
        for ttok in range(16):
            w0 = [t_gw, t_h2]
            for c in range(NCH):
                tok_mm = T(lambda g, c=c, ttok=ttok: g.matmul(
                    PB(5)[:, ttok * 8:ttok * 8 + 8],
                    areg[:, c * NT + ttok * 128:c * NT + (ttok + 1) * 128],
                    gatebuf[:, c * E:(c + 1) * E],
                    start=(c == 0), stop=(c == NCH - 1)),
                    waits=w0 if (c == 0 and ttok == 0) else (),
                    inc="T" if (c == NCH - 1 and ttok == 15) else None)
        GP3 = PB(5)[:, 0:128].rearrange("p (n w) -> p n w", w=8)
        GS1 = s8k[:, 0:128]
        GP2 = s8k[:, 128:256]
        GT1 = s8k[:, 256:384]
        wg3 = lambda apx: apx.rearrange("p (n w) -> p n w", w=8)
        V(lambda g: g.reduce_max(gcol[:, 0:16], GP3, axis=AX.X),
          waits=[tok_mm], sm=True)
        V(lambda g: g.scalar_tensor_tensor(
            wg3(GS1), GP3, 1.0,
            gcol[:, 0:16].unsqueeze(2).broadcast_to([128, 16, 8]),
            op0=ALU.mult, op1=ALU.subtract), sm=True)
        bank_free[5] = p.mark("vector")
        texp = Sc(lambda g: g.activation(out=GP2, in_=GS1, func=AF.Exp),
                  waits=[bank_free[5]], inc="S")
        V(lambda g: g.tensor_single_scalar(GT1, GP2, 1.0, op=ALU.is_ge),
          waits=[texp], sm=True)
        V(lambda g: g.scalar_tensor_tensor(
            GT1, GT1, -1e9, GP2, op0=ALU.mult, op1=ALU.add), sm=True)
        V(lambda g: g.reduce_max(gcol[:, 0:16], wg3(GT1), axis=AX.X), sm=True)
        V(lambda g: g.tensor_tensor(
            wg3(GT1), wg3(GP2),
            gcol[:, 0:16].unsqueeze(2).broadcast_to([128, 16, 8]),
            op=ALU.is_ge), sm=True)
        V(lambda g: g.tensor_scalar_add(gcol[:, 0:16], gcol[:, 0:16], 1.0),
          sm=True)
        V(lambda g: g.reciprocal(gcol[:, 0:16], gcol[:, 0:16]), sm=True)
        V(lambda g: g.tensor_mul(GP2, GP2, GT1), sm=True)
        twf = V(lambda g: g.scalar_tensor_tensor(
            wg3(GP2), wg3(GP2), 1.0,
            gcol[:, 0:16].unsqueeze(2).broadcast_to([128, 16, 8]),
            op0=ALU.mult, op1=ALU.mult), sm=True)
        for ttok in range(16):
            wts = [twf, t_idnf]
            if bank_free[6] is not None:
                wts.append(bank_free[6])
            ttr = T(lambda g, ttok=ttok: g.transpose(
                PB(6)[0:8, 0:128], GP2[:, ttok * 8:ttok * 8 + 8], idnf[:]),
                waits=wts, inc="T")
            tcp = V(lambda g, ttok=ttok: g.tensor_copy(
                wfT[:, ttok * 128:(ttok + 1) * 128], PB(6)[0:8, 0:128]),
                waits=[ttr], sm=True)
            tcp2 = V(lambda g, ttok=ttok: g.tensor_copy(
                R_WF[:, ttok * 128:(ttok + 1) * 128], PB(6)[0:8, 0:128]),
                sm=True)
            bank_free[6] = tcp2
        phase_barrier()

    if phases >= 8 and not SPARSE:
        bank_free = {b: None for b in range(8)}
        w13_free = {0: None, 1: None}
        w2_free = None
        gw_free = None
        silu_free = {0: None, 1: None}
        for e in range(E):
            wts = [] if w2_free is None else [w2_free]
            t_w2 = DG(lambda g, e=e: g.dma_start(
                out=w2buf[:].rearrange("p (f o) -> p f o", o=1024),
                in_=dr["w2G"].ap().rearrange("(e p) (f o) -> e p f o",
                                             p=128, o=1024)[e]),
                waits=wts)
            for th in range(NTT):
                wts = [t_esel[e]]
                if bank_free[6] is not None:
                    wts.append(bank_free[6])
                tok_wbc = None
                for tb_ in range(4):
                    tok_wbc = T(lambda g, e=e, th=th, tb_=tb_: g.matmul(
                        PB(6)[:, tb_ * 128:(tb_ + 1) * 128],
                        esel[:, e * 128:(e + 1) * 128],
                        wfT[:, th * 512 + tb_ * 128:th * 512 + (tb_ + 1) * 128],
                        start=True, stop=True),
                        waits=wts if tb_ == 0 else (), inc="T" if tb_ == 3 else None)
                for og in range(4):
                    par = og % 2
                    wts = [] if w13_free[par] is None else [w13_free[par]]
                    t_w1 = DG(lambda g, e=e, og=og, par=par: g.dma_start(
                        out=w13buf[:, par * 8192:par * 8192 + 4096].rearrange(
                            "p (i o) -> p i o", o=512),
                        in_=dr["w13Gb%d" % og].ap().rearrange(
                            "(e p) (i o) -> e p i o", p=128, o=512)[e]),
                        waits=wts)
                    t_w3 = DG(lambda g, e=e, og=og, par=par: g.dma_start(
                        out=w13buf[:, par * 8192 + 4096:par * 8192 + 8192].rearrange(
                            "p (i o) -> p i o", o=512),
                        in_=dr["w13Gb%d" % (4 + og)].ap().rearrange(
                            "(e p) (i o) -> e p i o", p=128, o=512)[e]),
                        waits=wts)
                    last_mm = None
                    for fc in range(4):
                        f = og * 4 + fc
                        ub = fc % 2
                        wts_u = [t_w1]
                        if bank_free[ub * 2] is not None:
                            wts_u.append(bank_free[ub * 2])
                        tok_u = None
                        for i in range(NCH):
                            tok_u = T(lambda g, i=i, fc=fc, og=og, th=th, par=par, ub=ub: g.matmul(
                                PB(ub * 2),
                                w13buf[:, par * 8192 + i * 512 + fc * 128:
                                       par * 8192 + i * 512 + (fc + 1) * 128],
                                areg[:, i * NT + th * TT:i * NT + (th + 1) * TT],
                                start=(i == 0), stop=(i == NCH - 1)),
                                waits=wts_u if i == 0 else (),
                                inc="T" if i == NCH - 1 else None)
                        wts_v = [t_w3]
                        if bank_free[ub * 2 + 1] is not None:
                            wts_v.append(bank_free[ub * 2 + 1])
                        tok_v = None
                        for i in range(NCH):
                            tok_v = T(lambda g, i=i, fc=fc, og=og, th=th, par=par, ub=ub: g.matmul(
                                PB(ub * 2 + 1),
                                w13buf[:, par * 8192 + 4096 + i * 512 + fc * 128:
                                       par * 8192 + 4096 + i * 512 + (fc + 1) * 128],
                                areg[:, i * NT + th * TT:i * NT + (th + 1) * TT],
                                start=(i == 0), stop=(i == NCH - 1)),
                                waits=wts_v if i == 0 else (),
                                inc="T" if i == NCH - 1 else None)
                        sp = fc % 2
                        wsl = [tok_u]
                        if silu_free[sp] is not None:
                            wsl.append(silu_free[sp])
                        tsl = Sc(lambda g, sp=sp, ub=ub: g.activation(
                            out=silub[:, sp * TT:(sp + 1) * TT], in_=PB(ub * 2),
                            func=AF.Silu), waits=wsl, inc="S", nc_=True)
                        bank_free[ub * 2] = tsl
                        wgw = [tsl, tok_v]
                        if gw_free is not None and f == 0 and th == 0:
                            wgw.append(gw_free)
                        tg1 = V(lambda g, f=f, sp=sp, ub=ub: g.tensor_tensor(
                            gw[:, f * TT:(f + 1) * TT], PB(ub * 2 + 1),
                            silub[:, sp * TT:(sp + 1) * TT], op=ALU.mult),
                            waits=wgw, inc="V", nc_=True)
                        bank_free[ub * 2 + 1] = tg1
                        silu_free[sp] = tg1
                        tg2 = V(lambda g, f=f: g.tensor_tensor(
                            gw[:, f * TT:(f + 1) * TT], gw[:, f * TT:(f + 1) * TT],
                            PB(6)[:, 0:512], op=ALU.mult), waits=[tok_wbc], inc="V")
                        last_mm = tg2
                    w13_free[par] = last_mm
                bank_free[6] = p.mark("vector")
                t_gw_done = p.mark("vector")
                for oc2 in range(NCH):
                    bk = 4 + oc2 % 2
                    wts = [t_w2, t_gw_done]
                    if bank_free[bk] is not None:
                        wts.append(bank_free[bk])
                    tok_d = None
                    for f in range(16):
                        tok_d = T(lambda g, f=f, oc2=oc2, bk=bk: g.matmul(
                            PB(bk), w2buf[:, f * 1024 + oc2 * 128:f * 1024 + (oc2 + 1) * 128],
                            gw[:, f * TT:(f + 1) * TT],
                            start=(f == 0), stop=(f == 15)),
                            waits=wts if f == 0 else (),
                            inc="T" if f == 15 else None)
                    ev = V(lambda g, oc2=oc2, th=th, bk=bk: g.tensor_add(
                        acc[:, oc2 * NT + th * TT:oc2 * NT + (th + 1) * TT],
                        acc[:, oc2 * NT + th * TT:oc2 * NT + (th + 1) * TT],
                        PB(bk)), waits=[tok_d], inc="V", nc_=True)
                    bank_free[bk] = ev
                gw_free = p.mark("tensor")
            w2_free = p.mark("tensor")
        phase_barrier()

    if phases >= 8 and SPARSE:
        SS = int(_os.environ.get("SS", "9"))
        IOoA = bass.IndirectOffsetOnAxis
        t_spc = DS(lambda g: g.dma_start(out=spcS[:], in_=dr["spc"].ap()))
        IOTA = spcS[:, 0:1]


        if SS >= 2:
            MSK8 = stgtm[0:8, 0:2048]
            INCL8 = stgtm[0:8, 2048:4096]
            POS8 = pool[0:8, :]
            t_msk = V(lambda g: g.tensor_single_scalar(
                MSK8[:], R_WF[:], 0.0, op=ALU.is_gt), waits=[t_s0_T], inc="V")
            t_scan = V(lambda g: g.tensor_tensor_scan(
                INCL8[:], MSK8[:], MSK8[:], 0.0, op0=ALU.add, op1=ALU.bypass),
                inc="V")
            NCOL = INCL8[:, 2047:2048]
            t_cmp = V(lambda g: g.tensor_single_scalar(
                scolF[0:8, 48:56], spcS[0:8, 8:16], NCOL, op=ALU.is_le),
                waits=[t_spc], sm=True)
            t_T8 = V(lambda g: g.reduce_sum(
                scolF[0:8, 56:57], scolF[0:8, 48:56], axis=AX.X), sm=True)
            t_tri = T(lambda g: g.matmul(
                PB(7)[0:8, 0:1], spcS[0:8, 16:24], scolF[0:8, 56:57],
                start=True, stop=True), waits=[t_T8], inc="T")
            t_S8 = V(lambda g: g.tensor_scalar_mul(
                scolF[0:8, 57:58], PB(7)[0:8, 0:1], 512.0), waits=[t_tri], sm=True)
            t_Bsb = V(lambda g: g.tensor_copy(
                scolF[0:8, 64:65], PB(7)[0:8, 0:1]), sm=True)
            t_p1 = V(lambda g: g.tensor_sub(POS8[:], INCL8[:], MSK8[:]),
                     waits=[t_scan], inc="V")
            t_p2 = V(lambda g: g.tensor_single_scalar(
                POS8[:], POS8[:], scolF[0:8, 57:58], op=ALU.add),
                waits=[t_p1, t_S8], inc="V")
            t_p3 = V(lambda g: g.tensor_scalar_add(POS8[:], POS8[:], -60000.0),
                     waits=[t_p2], inc="V")
            t_p4 = V(lambda g: g.tensor_mul(POS8[:], POS8[:], MSK8[:]), inc="V")
            t_p5 = V(lambda g: g.tensor_scalar_add(POS8[:], POS8[:], 60000.0), inc="V")

            t_bt = T(lambda g: g.transpose(
                PB(7)[0:1, 32:40], scolF[0:8, 64:65], idnf[0:8, 0:8]),
                waits=[t_Bsb, t_idnf], inc="T")
            t_br = V(lambda g: g.tensor_copy(
                scolF[0:1, 65:73], PB(7)[0:1, 32:40]), waits=[t_bt], sm=True)
            t_bb = T(lambda g: g.matmul(
                PB(7)[0:15, 64:72], onesRF[0:1, 0:15], scolF[0:1, 65:73],
                start=True, stop=True), waits=[t_br, t_onesr], inc="T")
            t_c15 = V(lambda g: g.tensor_single_scalar(
                scolF[0:15, 88:96], PB(7)[0:15, 64:72], IOTA[0:15, 0:1],
                op=ALU.is_le), waits=[t_bb, t_spc], sm=True)
            t_eo = V(lambda g: g.reduce_sum(
                scolF[0:15, 87:88], scolF[0:15, 88:96], axis=AX.X), sm=True)
            t_eo2 = V(lambda g: g.tensor_scalar_add(
                scolF[0:15, 87:88], scolF[0:15, 87:88], -1.0), sm=True)
            t_et = T(lambda g: g.transpose(
                PB(7)[0:1, 48:63], scolF[0:15, 87:88], idnf[0:15, 0:15]),
                waits=[t_eo2], inc="T")
            t_er = V(lambda g: g.tensor_copy(
                scolF[0:1, 73:88], PB(7)[0:1, 48:63]), waits=[t_et], sm=True)
            t_eb = T(lambda g: g.matmul(
                PB(7)[:, 80:95], onesRF[0:1, :], scolF[0:1, 73:88],
                start=True, stop=True), waits=[t_er], inc="T")
            t_wx1 = V(lambda g: g.tensor_scalar_mul(
                wcolF[:, 0:15], PB(7)[:, 80:95], 128.0), waits=[t_eb], inc="V")
            t_wx2 = V(lambda g: g.tensor_single_scalar(
                wcolF[:, 0:15], wcolF[:, 0:15], IOTA, op=ALU.add), inc="V")
            t_wxi = DG(lambda g: g.dma_start(out=icol[:, 0:15], in_=wcolF[:, 0:15]),
                       waits=[t_wx2])
            b7_free = t_wx1
            t_tkv = DG(lambda g: g.dma_start(out=icol[:, 16:32], in_=spcS[:, 32:48]),
                       waits=[t_spc])

        if SS >= 3:
            b5_bank6 = None
            tP = tW = None
            for g16 in range(16):
                wtsP = [t_p5, t_idnf]
                if g16 == 0 and b5_bank6 is not None:
                    wtsP.append(b5_bank6)
                tP = T(lambda g, g16=g16: g.transpose(
                    PB(6)[:, g16 * 8:g16 * 8 + 8],
                    POS8[:, g16 * 128:(g16 + 1) * 128],
                    idnf[0:8, 0:8]), waits=wtsP, inc="T" if g16 == 15 else None)
            for g16 in range(16):
                tW = T(lambda g, g16=g16: g.transpose(
                    PB(6)[:, 128 + g16 * 8:128 + g16 * 8 + 8],
                    R_WF[:, g16 * 128:(g16 + 1) * 128], idnf[32:40, 32:40]),
                    waits=[t_idnf], inc="T" if g16 == 15 else None)
            b5_T_last = tW
            WT = stgtm[:, 0:128]
            PT = stgtm[:, 128:256]
            M1R = stgtm[:, 256:384]
            M2R = stgtm[:, 384:512]
            w3 = lambda apx: apx.rearrange("p (n w) -> p n w", w=8)
            cW = V(lambda g: g.tensor_copy(WT, PB(6)[:, 128:256]),
                   waits=[tW, t_p4], inc="V")
            cP = V(lambda g: g.tensor_copy(PT, PB(6)[:, 0:128]),
                   waits=[tP], sm=True)
            b5_bank6 = cP
            V(lambda g: g.reduce_max(scolF[:, 0:16], w3(WT), axis=AX.X), sm=True)
            V(lambda g: g.tensor_tensor(
                w3(M1R), w3(WT),
                scolF[:, 0:16].unsqueeze(2).broadcast_to([128, 16, 8]),
                op=ALU.is_ge), sm=True)
            V(lambda g: g.tensor_single_scalar(M2R, WT, 0.0, op=ALU.is_gt),
              sm=True)
            V(lambda g: g.tensor_sub(M2R, M2R, M1R), sm=True)
            V(lambda g: g.reduce_sum(scolF[:, 48:64], w3(WT), axis=AX.X), sm=True)
            V(lambda g: g.tensor_sub(scolF[:, 48:64], scolF[:, 48:64],
                                     scolF[:, 0:16]), sm=True)
            V(lambda g: g.tensor_mul(M1R, M1R, PT), sm=True)
            V(lambda g: g.reduce_sum(scolF[:, 16:32], w3(M1R), axis=AX.X),
              sm=True)
            V(lambda g: g.tensor_mul(M2R, M2R, PT), sm=True)
            ta2 = V(lambda g: g.reduce_sum(scolF[:, 32:48], w3(M2R), axis=AX.X),
                    sm=True)
            PV = icolP[:].rearrange("p (g k) -> p g k", k=4)
            V(lambda g: g.tensor_copy(
                PV[:, :, 0:1], icol[:, 16:32].unsqueeze(2)),
                waits=[t_tkv], inc="V")
            V(lambda g: g.tensor_copy(
                PV[:, :, 1:2], scolF[:, 0:16].bitcast(I32).unsqueeze(2)),
                sm=True)
            V(lambda g: g.tensor_copy(
                PV[:, :, 2:3], icol[:, 16:32].unsqueeze(2)), sm=True)
            c4 = V(lambda g: g.tensor_copy(
                PV[:, :, 3:4], scolF[:, 48:64].bitcast(I32).unsqueeze(2)),
                sm=True)
            b5_v_done = c4
            tcA = DG(lambda g: g.dma_start(
                out=icol2[:, 0:32], in_=scolF[:, 16:48]), waits=[c4])
            tokflat_scats = []
            for g16 in range(16):
                s1 = DG(lambda g, g16=g16: g.indirect_dma_start(
                    out=pairflat_t.ap(),
                    out_offset=IOoA(icol2[:, g16:g16 + 1], 0),
                    in_=icolP[:, 4 * g16:4 * g16 + 2], in_offset=None,
                    bounds_check=None, oob_is_err=False),
                    waits=[tcA, t_zt])
                s2 = DG(lambda g, g16=g16: g.indirect_dma_start(
                    out=pairflat_t.ap(),
                    out_offset=IOoA(icol2[:, 16 + g16:17 + g16], 0),
                    in_=icolP[:, 4 * g16 + 2:4 * g16 + 4], in_offset=None,
                    bounds_check=None, oob_is_err=False), waits=[tcA])
                tokflat_scats += [s1, s2]
            wflat_scats = tokflat_scats
            tile_w2 = {}
            tile_w13 = {}
            tile_w2[0] = [DG(lambda g, h_=h_: g.indirect_dma_start(
                out=w2buf[:, h_ * 8192:(h_ + 1) * 8192], out_offset=None,
                in_=w2B_t[h_].ap(),
                in_offset=IOoA(icol[:, 0:1], 0), bounds_check=None,
                oob_is_err=False), waits=[t_wxi, wcast_toks["w2%d" % h_]])
                for h_ in range(2)]
            for og in range(2):
                tile_w13[(0, og)] = (
                    DG(lambda g, og=og: g.indirect_dma_start(
                        out=w13buf[:, og % 2 * 8192:og % 2 * 8192 + 4096],
                        out_offset=None,
                        in_=w13B_t[og].ap(),
                        in_offset=IOoA(icol[:, 0:1], 0), bounds_check=None,
                        oob_is_err=False), waits=[t_wxi]),
                    DG(lambda g, og=og: g.indirect_dma_start(
                        out=w13buf[:, og % 2 * 8192 + 4096:og % 2 * 8192 + 8192],
                        out_offset=None,
                        in_=w13B_t[4 + og].ap(),
                        in_offset=IOoA(icol[:, 0:1], 0), bounds_check=None,
                        oob_is_err=False), waits=[t_wxi]))

            gidx_toks = {}
            for ti in range(NTILES):
                for q in range(4):
                    gidx_toks[(ti, q)] = DS(lambda g, ti=ti, q=q: g.dma_start(
                        out=icol3[:, 2 * (ti * 4 + q):2 * (ti * 4 + q) + 2],
                        in_=pairflat_t.ap().rearrange(
                            "(a p) c -> a p c", p=128)[ti * 4 + q]),
                        waits=[t_zt] + tokflat_scats)
            b5_T_last = p.mark("tensor")

        if SS >= 4:
            bank_free = {b: None for b in range(8)}
            bank_free[0] = s0_bank[0]
            bank_free[1] = s0_bank[1]
            w13_free = {0: None, 1: None}
            w2_free = None
            silu_free = {0: last_sil_rd, 1: last_sil_rd}
            h2gfm_free = {0: None, 1: None}
            h2gtm_free = {0: None, 1: None}
            wrow_free = {0: None, 1: None}
            wbcsb_free = {0: b5_T_last, 1: b5_T_last}
            d2s_free = {0: None, 1: None}
            stg_dma_prev = []
            stage_dmas_all = []
            bank6_rd = b5_bank6

            def emit_h2g(tj):
                nonlocal b7_free
                ppj = tj % 2
                ev_h2g = None
                for q in range(4):
                    sl = q % 2
                    hg_w = [gidx_toks[(tj, q)]] + h2tm_dmas
                    if h2gtm_free[sl] is not None:
                        hg_w.append(h2gtm_free[sl])
                    t_hg = DG(lambda g, tj=tj, q=q, sl=sl: g.indirect_dma_start(
                        out=h2g_tm[:, sl * 1024:(sl + 1) * 1024], out_offset=None,
                        in_=h2tm_t.ap(),
                        in_offset=IOoA(
                            icol3[:, 2 * (tj * 4 + q):2 * (tj * 4 + q) + 1], 0),
                        bounds_check=None, oob_is_err=False), waits=hg_w)
                    tkt = None
                    for c in range(NCH):
                        twts = [t_hg, t_idn]
                        if c == 0 and b7_free is not None:
                            twts.append(b7_free)
                        tkt = T(lambda g, c=c, sl=sl: g.transpose(
                            psB[:, 7 * 512 + c * 64:7 * 512 + (c + 1) * 64].bitcast(BF16),
                            h2g_tm[:, sl * 1024 + c * 128:sl * 1024 + (c + 1) * 128],
                            IDN), waits=twts, inc="T" if c == NCH - 1 else None)
                    h2gtm_free[sl] = tkt
                    evw = [tkt]
                    if q == 0 and h2gfm_free[ppj] is not None:
                        evw.append(h2gfm_free[ppj])
                    if tj == 0 and q == 0:
                        evw.append(t_s0_T)
                    ev_h2g = V(lambda g, q=q, ppj=ppj: g.tensor_copy(
                        h2gfm[:, ppj * 4096:(ppj + 1) * 4096].rearrange(
                            "p (c j) -> p c j", j=512)[:, :, q * 128:(q + 1) * 128],
                        psB[:, 7 * 512:8 * 512].bitcast(BF16).rearrange(
                            "p (c j) -> p c j", j=128)), waits=evw, inc="V",
                        nc_=True)
                    b7_free = ev_h2g
                return ev_h2g

            def emit_w13_01(tj):
                for og0_ in range(2):
                    wts13 = [t_wxi, wcast_toks[og0_], wcast_toks[4 + og0_]]
                    if w13_free[og0_] is not None:
                        wts13.append(w13_free[og0_])
                    tile_w13[(tj, og0_)] = (
                        DG(lambda g, tj=tj, og=og0_: g.indirect_dma_start(
                            out=w13buf[:, og % 2 * 8192:og % 2 * 8192 + 4096],
                            out_offset=None, in_=w13B_t[og].ap(),
                            in_offset=IOoA(icol[:, tj:tj + 1], 0),
                            bounds_check=None, oob_is_err=False),
                            waits=wts13),
                        DG(lambda g, tj=tj, og=og0_: g.indirect_dma_start(
                            out=w13buf[:, og % 2 * 8192 + 4096:og % 2 * 8192 + 8192],
                            out_offset=None, in_=w13B_t[4 + og].ap(),
                            in_offset=IOoA(icol[:, tj:tj + 1], 0),
                            bounds_check=None, oob_is_err=False),
                            waits=wts13))

            t_h2gfm_cur = emit_h2g(0)
            for ti in range(NTILES):
                pp = ti % 2
                t_h2gfm = t_h2gfm_cur
                for og in range(4):
                    par = og % 2
                    if (ti, og) in tile_w13:
                        t_w1, t_w3 = tile_w13[(ti, og)]
                    else:
                        wts13 = [t_wxi]
                        if w13_free[par] is not None:
                            wts13.append(w13_free[par])
                        t_w1 = DG(lambda g, ti=ti, og=og, par=par: g.indirect_dma_start(
                            out=w13buf[:, par * 8192:par * 8192 + 4096],
                            out_offset=None,
                            in_=w13B_t[og].ap(),
                            in_offset=IOoA(icol[:, ti:ti + 1], 0), bounds_check=None,
                            oob_is_err=False), waits=wts13)
                        t_w3 = DG(lambda g, ti=ti, og=og, par=par: g.indirect_dma_start(
                            out=w13buf[:, par * 8192 + 4096:par * 8192 + 8192],
                            out_offset=None,
                            in_=w13B_t[4 + og].ap(),
                            in_offset=IOoA(icol[:, ti:ti + 1], 0), bounds_check=None,
                            oob_is_err=False), waits=wts13)
                    last_mm = None
                    for fc in range(4):
                        f = og * 4 + fc
                        ub = fc % 2
                        wts_u = [t_w1, t_h2gfm]
                        if bank_free[ub * 2] is not None:
                            wts_u.append(bank_free[ub * 2])
                        tok_u = None
                        for i in range(NCH):
                            tok_u = T(lambda g, i=i, fc=fc, par=par, ub=ub, pp=pp: g.matmul(
                                PB(ub * 2),
                                w13buf[:, par * 8192 + i * 512 + fc * 128:
                                       par * 8192 + i * 512 + (fc + 1) * 128],
                                h2gfm[:, pp * 4096 + i * 512:pp * 4096 + (i + 1) * 512],
                                start=(i == 0), stop=(i == NCH - 1)),
                                waits=wts_u if i == 0 else (),
                                inc="T" if i == NCH - 1 else None)
                        wts_v = [t_w3, t_h2gfm]
                        if bank_free[ub * 2 + 1] is not None:
                            wts_v.append(bank_free[ub * 2 + 1])
                        tok_v = None
                        for i in range(NCH):
                            tok_v = T(lambda g, i=i, fc=fc, par=par, ub=ub, pp=pp: g.matmul(
                                PB(ub * 2 + 1),
                                w13buf[:, par * 8192 + 4096 + i * 512 + fc * 128:
                                       par * 8192 + 4096 + i * 512 + (fc + 1) * 128],
                                h2gfm[:, pp * 4096 + i * 512:pp * 4096 + (i + 1) * 512],
                                start=(i == 0), stop=(i == NCH - 1)),
                                waits=wts_v if i == 0 else (),
                                inc="T" if i == NCH - 1 else None)
                        sp = fc % 2
                        wsl = [tok_u]
                        if silu_free[sp] is not None:
                            wsl.append(silu_free[sp])
                        tsl = Sc(lambda g, sp=sp, ub=ub: g.activation(
                            out=silub[:, sp * TT:(sp + 1) * TT], in_=PB(ub * 2),
                            func=AF.Silu), waits=wsl, inc="S", nc_=True)
                        bank_free[ub * 2] = tsl
                        tg1 = V(lambda g, f=f, sp=sp, ub=ub: g.tensor_tensor(
                            gw[:, f * TT:(f + 1) * TT], PB(ub * 2 + 1),
                            silub[:, sp * TT:(sp + 1) * TT], op=ALU.mult),
                            waits=[tsl, tok_v], inc="V", nc_=True)
                        bank_free[ub * 2 + 1] = tg1
                        silu_free[sp] = tg1
                        last_mm = tok_v
                    w13_free[par] = last_mm
                h2gfm_free[pp] = p.mark("tensor")
                if ti + 1 < NTILES:
                    t_h2gfm_cur = emit_h2g(ti + 1)
                    emit_w13_01(ti + 1)
                tww = None
                for q in range(4):
                    twts = [gidx_toks[(ti, q)], t_idnf]
                    if q == 0 and bank6_rd is not None:
                        twts.append(bank6_rd)
                    tww = T(lambda g, ti=ti, q=q: g.transpose(
                        PB(6)[0:1, q * 128:(q + 1) * 128],
                        icol3[:, 2 * (ti * 4 + q) + 1:
                              2 * (ti * 4 + q) + 2].bitcast(F32),
                        idnf[:]), waits=twts, inc="T" if q == 3 else None)
                cwr = V(lambda g, pp=pp: g.tensor_copy(
                    pool[64:65, pp * 512:(pp + 1) * 512], PB(6)[0:1, 0:512]),
                    waits=[tww] + ([wrow_free[pp]] if wrow_free[pp] is not None
                                   else []), inc="V", nc_=True)
                tok_wbc = T(lambda g, pp=pp: g.matmul(
                    PB(6), onesRF[64:65, :], pool[64:65, pp * 512:(pp + 1) * 512],
                    start=True, stop=True, tile_position=(64, 0)),
                    waits=[cwr], inc="T")
                wrow_free[pp] = tok_wbc
                wcw = [tok_wbc]
                if wbcsb_free[pp] is not None:
                    wcw.append(wbcsb_free[pp])
                cwbc = V(lambda g, pp=pp: g.tensor_copy(
                    pool[:, 1024 + pp * 512:1536 + pp * 512], PB(6)),
                    waits=wcw, inc="V", nc_=True)
                bank6_rd = cwbc
                t_gw_done = p.mark("vector")
                ev_stg = None
                tok_half = {}
                tok_d = {}
                for f in range(16):
                    for oc2 in range(NCH):
                        wts = ()
                        if f == 0:
                            wts = [tile_w2[ti][0], t_gw_done]
                            if bank_free[oc2] is not None:
                                wts.append(bank_free[oc2])
                        elif f == 8 and oc2 == 0:
                            wts = [tile_w2[ti][1]]
                        tok_d[oc2] = T(lambda g, f=f, oc2=oc2: g.matmul(
                            PB(oc2),
                            w2buf[:, f * 1024 + oc2 * 128:f * 1024 + (oc2 + 1) * 128],
                            gw[:, f * TT:(f + 1) * TT],
                            start=(f == 0), stop=(f == 15)),
                            waits=wts,
                            inc="T" if f in (7, 15) else None)
                    if f == 7:
                        tok_half[0] = tok_d[NCH - 1]
                tok_half[1] = tok_d[NCH - 1]
                if ti + 1 < NTILES:
                    tile_w2[ti + 1] = [None, None]
                    tile_w2[ti + 1][0] = DG(lambda g, ti=ti: g.indirect_dma_start(
                        out=w2buf[:, 0:8192], out_offset=None,
                        in_=w2B_t[0].ap(),
                        in_offset=IOoA(icol[:, ti + 1:ti + 2], 0),
                        bounds_check=None, oob_is_err=False),
                        waits=[t_wxi, wcast_toks["w20"], tok_half[0]])
                for oc2 in range(NCH):
                    sp2 = oc2 % 2
                    evw2 = [tok_d[oc2], cwbc]
                    if d2s_free[sp2] is not None:
                        evw2.append(d2s_free[sp2])
                    evd = V(lambda g, sp2=sp2, oc2=oc2, pp=pp: g.tensor_tensor(
                        d2s[:, sp2 * 512:(sp2 + 1) * 512], PB(oc2),
                        pool[:, 1024 + pp * 512:1536 + pp * 512],
                        op=ALU.mult), waits=evw2, inc="V", nc_=True)
                    wbcsb_free[pp] = evd
                    tbk = oc2
                    tdt = None
                    for q in range(4):
                        twts = [evd, t_idn]
                        tdt = T(lambda g, q=q, sp2=sp2, tbk=tbk: g.transpose(
                            psB[:, tbk * 512 + q * 64:tbk * 512 + (q + 1) * 64].bitcast(BF16),
                            d2s[:, sp2 * 512 + q * 128:sp2 * 512 + (q + 1) * 128],
                            IDN), waits=twts, inc="T" if q == 3 else None)
                    d2s_free[sp2] = tdt
                    evw3 = [tdt] + stg_dma_prev
                    if ti == 0 and oc2 == 0:
                        evw3 += [t_s0_T, t_p4, t_cmp, b5_v_done]
                    ev_stg = Sc(lambda g, oc2=oc2, tbk=tbk: g.activation(
                        out=stgtm[:].rearrange("p (q x) -> p q x", x=1024)[
                            :, :, oc2 * 128:(oc2 + 1) * 128],
                        in_=psB[:, tbk * 512:tbk * 512 + 256].bitcast(BF16).rearrange(
                            "p (q j) -> p q j", j=128),
                        func=AF.Copy), waits=evw3, inc="S", nc_=True)
                    bank_free[tbk] = ev_stg
                sd_toks = []
                for q in range(4):
                    sd_toks.append(DS(lambda g, ti=ti, q=q: g.dma_start(
                        out=stg_t.ap()[ti * 512 + q * 128:ti * 512 + (q + 1) * 128, :],
                        in_=stgtm[:, q * 1024:(q + 1) * 1024]), waits=[ev_stg]))
                stg_dma_prev = sd_toks
                stage_dmas_all += sd_toks
                w2_free = p.mark("tensor")
                if ti + 1 < NTILES:
                    wg = [t_wxi, wcast_toks["w21"], w2_free]
                    tile_w2[ti + 1][1] = DG(lambda g, ti=ti: g.indirect_dma_start(
                        out=w2buf[:, 8192:16384], out_offset=None,
                        in_=w2B_t[1].ap(),
                        in_offset=IOoA(icol[:, ti + 1:ti + 2], 0),
                        bounds_check=None, oob_is_err=False), waits=wg)

        if SS >= 5:
            h2gF = h2gfm[:].bitcast(F32)
            def comb_ap(g16, half):
                sl4 = g16 % 4
                buf = stgtm if sl4 < 2 else h2gF
                base = (sl4 % 2) * 2048 + half * 1024
                return buf[:, base:base + 1024]
            comb_free = {s: None for s in range(4)}
            ps3_free = {0: bank_free[0], 1: bank_free[1],
                        2: bank_free[2], 3: bank_free[3]}
            for g16 in range(16):
                sl4 = g16 % 4
                gwts = stage_dmas_all + stg_dma_prev
                if comb_free[sl4] is not None:
                    gwts = gwts + [comb_free[sl4]]
                if g16 < 4:
                    gwts = gwts + [h2gfm_free[0], h2gfm_free[1]]
                gA = DG(lambda g, g16=g16: g.indirect_dma_start(
                    out=comb_ap(g16, 0), out_offset=None,
                    in_=stg_t.ap(),
                    in_offset=IOoA(icol2[:, g16:g16 + 1], 0),
                    bounds_check=None, oob_is_err=False), waits=gwts)
                gB = DG(lambda g, g16=g16: g.indirect_dma_start(
                    out=comb_ap(g16, 1), out_offset=None,
                    in_=stg_t.ap(),
                    in_offset=IOoA(icol2[:, 16 + g16:17 + g16], 0),
                    bounds_check=None, oob_is_err=False), waits=gwts)
                va = V(lambda g, g16=g16: g.tensor_add(
                    comb_ap(g16, 0), comb_ap(g16, 0), comb_ap(g16, 1)),
                    waits=[gA, gB], inc="V", nc_=True)
                t3 = None
                for c in range(NCH):
                    bk3 = c % 4
                    off3 = (c // 4) * 128
                    twts = [va, t_idnf]
                    if ps3_free[bk3] is not None:
                        twts.append(ps3_free[bk3])
                    t3 = T(lambda g, c=c, g16=g16, bk3=bk3, off3=off3: g.transpose(
                        psB[:, bk3 * 512 + off3:bk3 * 512 + off3 + 128],
                        comb_ap(g16, 0)[:, c * 128:(c + 1) * 128],
                        idnf[:]), waits=twts, inc="T")
                    ad = V(lambda g, c=c, g16=g16, bk3=bk3, off3=off3: g.tensor_add(
                        acc[:, c * NT + g16 * 128:c * NT + (g16 + 1) * 128],
                        acc[:, c * NT + g16 * 128:c * NT + (g16 + 1) * 128],
                        psB[:, bk3 * 512 + off3:bk3 * 512 + off3 + 128]),
                        waits=[t3], inc="V", nc_=True)
                    ps3_free[bk3] = ad
                comb_free[sl4] = t3
            phase_barrier()
    t_out = DS(lambda g: g.dma_start(
        out=out_t.ap().rearrange("(c p) t -> p c t", p=128),
        in_=acc[:].rearrange("p (c t) -> p c t", c=NCH)))
    outs = [t_out]
    dbg_toks = []
    for (name, shape, dt_, fn) in dbg:
        dbg_toks.append(DG(lambda g, fn=fn, name=name: g.dma_start(
            out=dbg_t[name].ap(), in_=fn(ctx))))
    for t in outs:
        p.wait("sync", t)
    for t in dbg_toks:
        p.wait("gpsimd", t)
    for qeng, pool_sems in p.dma_pool.items():
        for s_ in pool_sems:
            u = p.dma_uses.get(s_, 0)
            if u:
                p.wait(qeng, (s_, 16 * u))

    with nc.Block() as block:
        p.emit(block)
    stack.close()
    nc._dbg = dict(ctx=ctx, dr=dr, out_t=out_t, dbg_t=dbg_t, sb=sb)
    return nc



def _rope_tables(positions):
    inv = 1.0 / (10000.0 ** (np.arange(0, HD, 2, dtype=np.float64) / HD))
    ang = inv[:, None] * positions[None, :].astype(np.float64)
    cos = np.tile(np.cos(ang), (4, 1)).astype(np.float32)
    sin = np.tile(np.sin(ang), (4, 1)).astype(np.float32)
    return np.ascontiguousarray(cos), np.ascontiguousarray(sin)


def _host_prep(inputs):
    x = np.asarray(inputs["L0_attn_x"])
    f32 = np.float32

    def T2(a):
        return np.ascontiguousarray(np.asarray(a).T.astype(f32, copy=False))

    qperm = []
    for (hA, hB) in Q_CHUNK_HEADS:
        qperm.extend(range(hA * HD, (hA + 1) * HD))
        qperm.extend(range(hB * HD, (hB + 1) * HD))
    qperm = np.array(qperm)
    wqkvB = np.asarray(inputs["L0_ffn_wqkv"])
    rows = np.concatenate([qperm, np.arange(1024, 1536)])
    wqkvB_T = np.ascontiguousarray(wqkvB[rows].T)
    wo = np.asarray(inputs["L0_ffn_wo"])
    wo_T = np.ascontiguousarray(wo[:, qperm].T)

    w13 = np.asarray(inputs["L0_ffn_w13"])
    w13_T = np.transpose(w13, (0, 2, 1)).astype(f32, copy=False)
    w2 = np.asarray(inputs["L0_ffn_w2"])
    w2_T = np.transpose(w2, (0, 2, 1)).astype(f32, copy=False)
    w13G = (w13_T.reshape(E, 8, 128, 8, 512).transpose(0, 2, 3, 1, 4)
            .reshape(E * 128, 8, 4096))
    w13Gb = {("w13Gb%d" % b_): np.ascontiguousarray(w13G[:, b_])
             for b_ in range(8)}
    w2G = np.ascontiguousarray(
        w2_T.reshape(E, 16, 128, 1024).transpose(0, 2, 1, 3)
        .reshape(E * 128, 16384))
    spc = np.zeros((128, 64), f32)
    spc[:, 0] = np.arange(128)
    for j in range(8):
        spc[:, 8 + j] = 512.0 * j + 1.0
    for k in range(8):
        for e_ in range(8):
            if k < e_:
                spc[k, 16 + e_] = 1.0
    for g_ in range(16):
        spc[:, 32 + g_] = np.arange(128) + 128.0 * g_

    idn = np.eye(128, dtype=f32)
    r2m = np.zeros((128, 128), f32)
    for b0 in (0, 64):
        for i in range(32):
            r2m[b0 + i + 32, b0 + i] = -1.0
            r2m[b0 + i, b0 + i + 32] = 1.0
    consts = np.zeros((128, 8), f32)
    consts[:, 0] = EPS; consts[:, 1] = 1e-10; consts[:, 2] = -1e9; consts[:, 3] = 1.0

    ln1 = np.ascontiguousarray(np.asarray(inputs["L0_ffn_ln1_w"]).reshape(NCH, 128).T)
    ln2 = np.ascontiguousarray(np.asarray(inputs["L0_ffn_ln2_w"]).reshape(NCH, 128).T)

    shared = dict(
        consts=consts, idn=idn, r2m=r2m,
        wqkvA_T=T2(inputs["L0_attn_Wqkv"]), woutA_T=T2(inputs["L0_attn_Wout"]),
        wqkvB_T=wqkvB_T, ln1=ln1, ln2=ln2, wo_T=wo_T,
        gate_T=T2(inputs["L0_ffn_gate_w"]), w2G=w2G, spc=spc, **w13Gb,
    )
    in_maps = []
    for core in range(8):
        b, s = core // 2, core % 2
        own = np.ascontiguousarray(x[b, s * NT:(s + 1) * NT].T)
        oth = np.ascontiguousarray(x[b, (1 - s) * NT:(2 - s) * NT].T)
        cos_o, sin_o = _rope_tables(np.arange(s * NT, (s + 1) * NT))
        cos_x, sin_x = _rope_tables(np.arange((1 - s) * NT, (2 - s) * NT))
        flag = np.full((128, 1), 1.0 if s == 0 else 0.0, f32)
        m = dict(shared)
        m.update(xT=own, xoT=oth, flag=flag,
                 cos_o=cos_o, sin_o=sin_o, cos_x=cos_x, sin_x=sin_x)
        in_maps.append(m)
    return in_maps


def kernel(**inputs):
    in_maps = _host_prep(inputs)
    nc = build_program()
    res = run_bass_kernel_spmd(nc, in_maps, core_ids=list(range(8)))
    out = np.empty((4, 4096, 1024), np.float32)
    for core in range(8):
        b, s = core // 2, core % 2
        out[b, s * NT:(s + 1) * NT] = res.results[core]["out"].T
    return out
